# version 34
# baseline (speedup 1.0000x reference)
import contextlib
import numpy as np

import concourse.bass as bass
import concourse.mybir as mybir
from concourse.bass_utils import run_bass_kernel_spmd

F32 = mybir.dt.float32
BF16 = mybir.dt.bfloat16
AF = mybir.ActivationFunctionType
ALU = mybir.AluOpType

D = 1024
HN = 16
NTOK = 2048
NT = 16
GT = 4
NG = NT // GT
H4 = 4096
EPS = 1e-5
PI = float(np.pi)
QO, KO, VO, UO = 0, D, 2 * D, 3 * D


def build_nc():
    nc = bass.Bass("TRN2", target_bir_lowering=False)

    x_h = nc.dram_tensor("x", [NTOK, D], F32, kind="ExternalInput")
    wq_h = nc.dram_tensor("wq", [128, 8 * H4], F32, kind="ExternalInput")
    wo_h = nc.dram_tensor("wo", [128, 8 * D], F32, kind="ExternalInput")
    w1_h = nc.dram_tensor("w1", [128, 32 * D], F32, kind="ExternalInput")
    w2_h = nc.dram_tensor("w2", [128, 32 * D], F32, kind="ExternalInput")
    bq_h = nc.dram_tensor("bq", [1, H4], F32, kind="ExternalInput")
    bo_h = nc.dram_tensor("bo", [1, D], F32, kind="ExternalInput")
    b1_h = nc.dram_tensor("b1", [1, H4], F32, kind="ExternalInput")
    b1c_h = nc.dram_tensor("b1c", [128, 32], F32, kind="ExternalInput")
    b2_h = nc.dram_tensor("b2", [1, D], F32, kind="ExternalInput")
    lnw_h = nc.dram_tensor("lnw", [128, D], F32, kind="ExternalInput")
    lnb_h = nc.dram_tensor("lnb", [128, D], F32, kind="ExternalInput")
    th_h = nc.dram_tensor("th", [128, D], F32, kind="ExternalInput")
    tc_h = nc.dram_tensor("tc", [128, NT], F32, kind="ExternalInput")
    sc_h = nc.dram_tensor("sc", [128, 1], F32, kind="ExternalInput")
    id_h = nc.dram_tensor("idm", [128, 128], F32, kind="ExternalInput")
    out_h = nc.dram_tensor("y", [NTOK, D], F32, kind="ExternalOutput")
    qp_scr = nc.dram_tensor("qp_scr", [NTOK, 2048], BF16, kind="Internal")
    u_scr = nc.dram_tensor("u_scr", [NTOK, D], BF16, kind="Internal")
    w1_scr = nc.dram_tensor("w1_scr", [128, 32 * D], BF16, kind="Internal")
    w2_scr = nc.dram_tensor("w2_scr", [128, 32 * D], BF16, kind="Internal")

    es = contextlib.ExitStack()
    with es:
        E = lambda cm: es.enter_context(cm)
        ident = E(nc.sbuf_tensor("ident", [128, 128], BF16))
        scol = E(nc.sbuf_tensor("scol", [128, 1], F32))
        lnw_rep = E(nc.sbuf_tensor("lnw_rep", [128, D], BF16))
        lnb_rep = E(nc.sbuf_tensor("lnb_rep", [128, D], BF16))
        ones_row = E(nc.sbuf_tensor("ones_row", [1, 512], BF16))
        pi_col = E(nc.sbuf_tensor("pi_col", [128, 1], F32))
        p32_col = E(nc.sbuf_tensor("p32_col", [128, 1], F32))
        npi_col = E(nc.sbuf_tensor("npi_col", [128, 1], F32))
        eps_col = E(nc.sbuf_tensor("eps_col", [128, 1], F32))
        wo_bf = E(nc.sbuf_tensor("wo_bf", [128, 8 * D], BF16))
        w1sl = E(nc.sbuf_tensor("w1sl", [128, 3, D], BF16))
        s_sb = E(nc.sbuf_tensor("s_sb", [128, D], BF16))
        mv_c = E(nc.sbuf_tensor("mv_c", [128, 2, 2], F32))
        sd_c = E(nc.sbuf_tensor("sd_c", [128, 2], F32))
        inv_c = E(nc.sbuf_tensor("inv_c", [128, 2], F32))
        nmi_c = E(nc.sbuf_tensor("nmi_c", [128, 2], F32))
        bn6 = E(nc.sbuf_tensor("bn6", [128, 2, 12], F32))
        ps_a = E(nc.psum_tensor("ps_a", [128, D], F32))
        ps_b = E(nc.psum_tensor("ps_b", [128, D], F32))
        ps_c = E(nc.psum_tensor("ps_c", [128, D], F32))

        sems = {}
        for n in ["pro", "proc", "wql", "wol",
                  "xl", "sbn", "ssd", "siv", "sxn", "sxnt", "sxnf",
                  "sk4", "sv4", "su4", "sq4", "sks", "sqs",
                  "stbl", "smod", "ssc", "skp", "sqp",
                  "sve", "sue", "ssm", "st1", "st2", "se1v", "sqpf",
                  "sust", "sqst", "w1c", "w1s",
                  "w2l", "ssbv", "ql", "ul", "xl2",
                  "somm", "sbn2", "ssd2", "siv2", "so1", "ssy",
                  "wq0", "wq1", "wq2", "wq3",
                  "syt", "syte", "swom", "sx1e",
                  "sbn3", "ssd3", "siv3", "sxn2", "sxn2t", "sxn2f",
                  "se1", "se2", "sodd", "sd4", "sd4e", "sd4r",
                  "sw1mg", "sw1ld", "sw1cc", "sgel", "shd", "shde",
                  "sw2t", "sw2e", "sout", "brl",
                  "xl0", "xl1", "su0", "su1", "sq0", "sq1", "ws0", "ws1",
                  "ql0", "ql1", "xa0", "xa1", "wl0", "wl1", "wl2",
                  "sot0", "sot1", "sot2", "sot3"]:
            sems[n] = E(nc.semaphore("s_" + n))
        h = lambda n: sems[n]

        PRO_TOT = 16 * 7

        s1 = contextlib.ExitStack()
        with s1:
            E1 = lambda cm: s1.enter_context(cm)
            wq_bf = E1(nc.sbuf_tensor("wq_bf", [128, 8 * H4], BF16))
            bq_row = E1(nc.sbuf_tensor("bq_row", [1, H4], BF16))
            th_rep = E1(nc.sbuf_tensor("th_rep", [128, D], F32))
            tcol = E1(nc.sbuf_tensor("tcol", [128, NT], F32))
            xbuf = E1(nc.sbuf_tensor("xbuf", [128, 2, D], F32))
            xn_tm = E1(nc.sbuf_tensor("xn_tm", [128, 2, D], BF16))
            xn_fm = E1(nc.sbuf_tensor("xn_fm", [128, 2, D], BF16))
            ks_sb = E1(nc.sbuf_tensor("ks_sb", [128, 2, D], BF16))
            qs_sb = E1(nc.sbuf_tensor("qs_sb", [128, 2, D], BF16))
            v_sb = E1(nc.sbuf_tensor("v_sb", [128, 2, D], BF16))
            u_sb = E1(nc.sbuf_tensor("u_sb", [128, 2, D], BF16))
            ang_sb = E1(nc.sbuf_tensor("ang_sb", [128, 2, 2, D], F32))
            dlt_sb = E1(nc.sbuf_tensor("dlt_sb", [128, D], F32))
            ki_sb = E1(nc.sbuf_tensor("ki_sb", [128, D], mybir.dt.int32))
            cos_sb = E1(nc.sbuf_tensor("cos_sb", [128, 2, D], BF16))
            sin_sb = E1(nc.sbuf_tensor("sin_sb", [128, 2, D], BF16))
            kp_sb = E1(nc.sbuf_tensor("kp_sb", [128, 2, 2048], BF16))
            qp_sb = E1(nc.sbuf_tensor("qp_sb", [128, 2, 2048], BF16))
            qf_sb = E1(nc.sbuf_tensor("qf_sb", [128, 2, 2048], BF16))
            w1st = E1(nc.sbuf_tensor("w1st", [128, 2, D], BF16))
            s_ps = E1(nc.psum_tensor("s_ps", [128, D], F32))

            hview = lambda ap: ap.rearrange("p (hh e) -> p hh e", hh=HN)

            blk1 = nc.Block("b1")
            blk1.__enter__()

            @blk1.gpsimd
            def _(g):
                g.memset(ones_row[:], 1.0)
                g.memset(pi_col[:], 2 * PI)
                g.memset(p32_col[:], 0.5 * PI)
                g.memset(npi_col[:], -2 * PI)
                g.memset(eps_col[:], EPS).then_inc(h("proc"), 1)
                g.dma_start(out=th_rep[:], in_=th_h[:]).then_inc(h("pro"), 16)
                g.dma_start(out=tcol[:], in_=tc_h[:]).then_inc(h("pro"), 16)
                g.dma_start(out=scol[:], in_=sc_h[:]).then_inc(h("pro"), 16)
                g.dma_start(out=ident[:], in_=id_h[:]).then_inc(h("pro"), 16)
                g.dma_start(out=lnw_rep[:], in_=lnw_h[:]).then_inc(h("pro"), 16)
                g.dma_start(out=lnb_rep[:], in_=lnb_h[:]).then_inc(h("pro"), 16)
                g.dma_start(out=bq_row[:], in_=bq_h[:]).then_inc(h("pro"), 16)
                for wchunk in range(4):
                    g.dma_start(out=wq_bf[:, wchunk * 2 * H4:(wchunk + 1) * 2 * H4],
                                in_=wq_h[:, wchunk * 2 * H4:(wchunk + 1) * 2 * H4]).then_inc(h("wq%d" % wchunk), 16)
                g.dma_start(out=wo_bf[:], in_=wo_h[:]).then_inc(h("wol"), 16)
                for fc in range(64):
                    sl = fc % 2
                    src = w1_h if fc < 32 else w2_h
                    dst = w1_scr if fc < 32 else w2_scr
                    c = fc % 32
                    if fc >= 2:
                        g.wait_ge(h("ws%d" % sl), 16 * (fc // 2))
                    g.dma_start(out=w1st[:, sl, :], in_=src[:, c * D:(c + 1) * D]).then_inc(h("w1c"), 16)
                    g.wait_ge(h("w1c"), 16 * (fc + 1))
                    g.dma_start(out=dst[:, c * D:(c + 1) * D], in_=w1st[:, sl, :]).then_inc(h("ws%d" % sl), 16)
                g.wait_ge(h("ws0"), 16 * 32)
                g.wait_ge(h("ws1"), 16 * 32)

            @blk1.sync
            def _(sy):
                for t in range(NT + 1):
                    if t < NT:
                        if t >= 2:
                            sy.wait_ge(h("sxn"), t - 1)
                        sy.dma_start(out=xbuf[:, t % 2, :],
                                     in_=x_h[t * 128:(t + 1) * 128, :]).then_inc(h("xl%d" % (t % 2)), 16)
                    if t >= 1:
                        tp = t - 1
                        sy.wait_ge(h("sue"), tp + 1)
                        sy.dma_start(out=u_scr[tp * 128:(tp + 1) * 128, :],
                                     in_=u_sb[:, tp % 2, :]).then_inc(h("su%d" % (tp % 2)), 16)
                        sy.wait_ge(h("sqpf"), tp + 1)
                        sy.dma_start(out=qp_scr[tp * 128:(tp + 1) * 128, :],
                                     in_=qf_sb[:, tp % 2, :]).then_inc(h("sq%d" % (tp % 2)), 16)
                sy.wait_ge(h("ws0"), 16)
                sy.dma_start(out=w1sl[:, 0, :], in_=w1_scr[:, 0:D]).then_inc(h("wl0"), 16)
                sy.wait_ge(h("ws1"), 16)
                sy.dma_start(out=w1sl[:, 1, :], in_=w1_scr[:, D:2 * D]).then_inc(h("wl1"), 16)
                sy.wait_ge(h("su0"), 16 * 8)
                sy.wait_ge(h("su1"), 16 * 8)
                sy.wait_ge(h("sq0"), 16 * 8)
                sy.wait_ge(h("sq1"), 16 * 8)

            @blk1.vector
            def _(v):
                v.wait_ge(h("pro"), PRO_TOT)
                v.wait_ge(h("proc"), 1)
                v.tensor_scalar_mul(dlt_sb[:], th_rep[:], 6.4206382432985265e-09)
                v.drain()
                v.scalar_tensor_tensor(th_rep[:], th_rep[:], 0.15915493667125702,
                                       dlt_sb[:], ALU.mult, ALU.add).then_inc(h("smod"), 1)

                def prep_a(t):
                    sl = t % 2
                    v.wait_ge(h("xl%d" % sl), 16 * (t // 2 + 1))
                    v.bn_stats(bn6[:, sl, 0:6], xbuf[:, sl, 0:512])
                    v.bn_stats(bn6[:, sl, 6:12], xbuf[:, sl, 512:1024])
                    v.drain()
                    v.bn_aggr(mv_c[:, sl, :], bn6[:, sl, :]).then_inc(h("sbn"), 1)

                def prep_b(t):
                    sl = t % 2
                    v.wait_ge(h("ssd"), t + 1)
                    v.reciprocal(inv_c[:, sl:sl + 1], sd_c[:, sl:sl + 1])
                    v.drain()
                    v.tensor_scalar(nmi_c[:, sl:sl + 1], mv_c[:, sl, 0:1],
                                    inv_c[:, sl:sl + 1], -1.0, ALU.mult, ALU.mult).then_inc(h("siv"), 1)

                def prep(t):
                    prep_a(t)
                    prep_b(t)

                def wrap_cos(sl):
                    v.scalar_tensor_tensor(ang_sb[:, sl, 1, :], ang_sb[:, sl, 0, :], 0.5,
                                           ang_sb[:, sl, 0, :], ALU.is_ge, ALU.subtract)
                    v.drain()
                    v.scalar_tensor_tensor(ang_sb[:, sl, 0, :], ang_sb[:, sl, 1, :], 0.5,
                                           ang_sb[:, sl, 1, :], ALU.is_ge, ALU.subtract)
                    v.drain()
                    v.scalar_tensor_tensor(ang_sb[:, sl, 1, :], ang_sb[:, sl, 0, :], 0.25,
                                           ang_sb[:, sl, 0, :], ALU.is_ge, ALU.subtract)
                    v.drain(fusable=True).then_inc(h("smod"), 1)

                def chain0():
                    v.wait_ge(h("stbl"), 1)
                    v.tensor_copy(ki_sb[:], ang_sb[:, 0, 0, :])
                    v.drain()
                    v.tensor_copy(ang_sb[:, 0, 1, :], ki_sb[:])
                    v.drain()
                    v.scalar_tensor_tensor(ang_sb[:, 0, 0, :], ang_sb[:, 0, 1, :], -1.0,
                                           ang_sb[:, 0, 0, :], ALU.mult, ALU.add)
                    v.drain()
                    wrap_cos(0)
                    v.tensor_scalar_mul(ang_sb[:, 1, 0, :], th_rep[:], 128.0)
                    v.drain()
                    v.tensor_copy(ki_sb[:], ang_sb[:, 1, 0, :])
                    v.drain()
                    v.tensor_copy(dlt_sb[:], ki_sb[:])
                    v.drain()
                    v.scalar_tensor_tensor(dlt_sb[:], dlt_sb[:], -1.0,
                                           ang_sb[:, 1, 0, :], ALU.mult, ALU.add)
                    v.drain()
                    v.scalar_tensor_tensor(ang_sb[:, 1, 0, :], dlt_sb[:], 0.5,
                                           dlt_sb[:], ALU.is_ge, ALU.subtract)
                    v.drain()
                    v.scalar_tensor_tensor(dlt_sb[:], ang_sb[:, 1, 0, :], 0.5,
                                           ang_sb[:, 1, 0, :], ALU.is_ge, ALU.subtract)
                    v.drain()

                def chain(t):
                    sl = t % 2
                    if t >= 2:
                        v.wait_ge(h("ssc"), t - 1)
                    v.tensor_add(ang_sb[:, sl, 0, :], ang_sb[:, 1 - sl, 0, :], dlt_sb[:])
                    v.drain()
                    wrap_cos(sl)

                prep(0)
                chain0()
                for t in range(NT):
                    sl = t % 2
                    if t + 1 < NT:
                        prep_a(t + 1)
                        prep_b(t + 1)
                    v.wait_ge(h("sks"), t + 1)
                    v.wait_ge(h("ssc"), t + 1)
                    if t >= 2:
                        v.wait_ge(h("ssm"), t - 1)
                    v.tensor_mul(hview(kp_sb[:, sl, :])[:, :, 0:64],
                                 hview(ks_sb[:, sl, :]), hview(cos_sb[:, sl, :]))
                    v.tensor_mul(hview(kp_sb[:, sl, :])[:, :, 64:128],
                                 hview(ks_sb[:, sl, :]), hview(sin_sb[:, sl, :])).then_inc(h("skp"), 1)
                    v.wait_ge(h("sv4"), t + 1)
                    v.tensor_copy(v_sb[:, sl, :], ps_c[:]).then_inc(h("sve"), 1)
                    v.wait_ge(h("su4"), t + 1)
                    if t >= 2:
                        v.wait_ge(h("su%d" % sl), 16 * (t // 2))
                    v.tensor_copy(u_sb[:, sl, :], ps_b[:]).then_inc(h("sue"), 1)
                    v.wait_ge(h("sqs"), t + 1)
                    if t >= 2:
                        v.wait_ge(h("st2"), t - 1)
                    v.tensor_mul(hview(qp_sb[:, sl, :])[:, :, 0:64],
                                 hview(qs_sb[:, sl, :]), hview(cos_sb[:, sl, :]))
                    v.tensor_mul(hview(qp_sb[:, sl, :])[:, :, 64:128],
                                 hview(qs_sb[:, sl, :]), hview(sin_sb[:, sl, :])).then_inc(h("sqp"), 1)
                    v.wait_ge(h("st1"), t + 1)
                    if t >= 2:
                        v.wait_ge(h("sq%d" % sl), 16 * (t // 2))
                    v.tensor_copy(qf_sb[:, sl, 0:1024], ps_b[:]).then_inc(h("se1v"), 1)
                    v.wait_ge(h("st2"), t + 1)
                    v.tensor_copy(qf_sb[:, sl, 1024:2048], ps_c[:]).then_inc(h("sqpf"), 1)
                    if t + 1 < NT:
                        chain(t + 1)

            @blk1.scalar
            def _(s):
                s.wait_ge(h("pro"), PRO_TOT)
                s.wait_ge(h("proc"), 1)
                s.wait_ge(h("smod"), 1)
                s.activation(ang_sb[:, 0, 0, :], th_rep[:], AF.Identity,
                             scale=tcol[:, 0:1]).then_inc(h("stbl"), 1)
                s.wait_ge(h("sbn"), 1)
                s.activation(sd_c[:, 0:1], mv_c[:, 0, 1:2], AF.Sqrt,
                             bias=eps_col[:, 0:1]).then_inc(h("ssd"), 1)
                for t in range(NT):
                    sl = t % 2
                    s.wait_ge(h("siv"), t + 1)
                    s.wait_ge(h("xl%d" % sl), 16 * (t // 2 + 1))
                    if t >= 2:
                        s.wait_ge(h("sxnt"), t - 1)
                    s.activation(xn_tm[:, sl, :], xbuf[:, sl, :], AF.Identity,
                                 bias=nmi_c[:, sl:sl + 1], scale=inv_c[:, sl:sl + 1]).then_inc(h("sxn"), 1)
                    s.wait_ge(h("sxnt"), t + 1)
                    if t >= 2:
                        s.wait_ge(h("sq4"), t - 1)
                    s.activation(xn_fm[:, sl, 0:512], ps_a[:, 0:512],
                                 AF.Identity).then_inc(h("sxnf"), 1)
                    s.activation(xn_fm[:, sl, 512:1024], ps_a[:, 512:1024],
                                 AF.Identity).then_inc(h("sxnf"), 1)
                    if t + 1 < NT:
                        slp = (t + 1) % 2
                        s.wait_ge(h("sbn"), t + 2)
                        s.activation(sd_c[:, slp:slp + 1], mv_c[:, slp, 1:2], AF.Sqrt,
                                     bias=eps_col[:, 0:1]).then_inc(h("ssd"), 1)
                    s.wait_ge(h("smod"), t + 2)
                    s.activation(sin_sb[:, sl, :], ang_sb[:, sl, 0, :], AF.Sin, scale=pi_col[:, 0:1])
                    s.activation(cos_sb[:, sl, :], ang_sb[:, sl, 1, :], AF.Sin,
                                 bias=p32_col[:, 0:1], scale=npi_col[:, 0:1]).then_inc(h("ssc"), 1)
                    s.wait_ge(h("sk4"), t + 1)
                    if t >= 2:
                        s.wait_ge(h("skp"), t - 1)
                    s.activation(ks_sb[:, sl, :], ps_b[:], AF.Silu).then_inc(h("sks"), 1)
                    s.wait_ge(h("sq4"), t + 1)
                    if t >= 2:
                        s.wait_ge(h("sqp"), t - 1)
                    s.activation(qs_sb[:, sl, :], ps_c[:], AF.Silu).then_inc(h("sqs"), 1)


            @blk1.tensor
            def _(t_):
                t_.wait_ge(h("pro"), PRO_TOT)
                t_.wait_ge(h("proc"), 1)

                def quarter(qoff, sl, psd, sem, first, xnf_t=None):
                    for seg in range(2):
                        for kc in range(8):
                            if first and seg == 0:
                                t_.wait_ge(h("wq%d" % (kc // 2)), 16)
                            if xnf_t is not None and seg == 0 and kc in (0, 4):
                                t_.wait_ge(h("sxnf"), 2 * xnf_t + 1 + (kc // 4))
                            t_.matmul(psd[:, seg * 512:(seg + 1) * 512],
                                      xn_fm[:, sl, kc * 128:(kc + 1) * 128],
                                      wq_bf[:, kc * H4 + qoff + seg * 512:
                                            kc * H4 + qoff + (seg + 1) * 512],
                                      start=(kc == 0), stop=False)
                        mm = t_.matmul(psd[:, seg * 512:(seg + 1) * 512],
                                       ones_row[0:1, 0:128],
                                       bq_row[0:1, qoff + seg * 512:qoff + (seg + 1) * 512],
                                       start=False, stop=True)
                    mm.then_inc(sem, 1)

                def xnt_mm(t):
                    sl = t % 2
                    t_.wait_ge(h("sxn"), t + 1)
                    if t >= 1:
                        t_.wait_ge(h("sxnf"), 2 * t)
                    for c in range(8):
                        mm = t_.matmul(ps_a[:, c * 128:(c + 1) * 128],
                                       xn_tm[:, sl, c * 128:(c + 1) * 128], ident[:],
                                       start=True, stop=True)
                    mm.then_inc(h("sxnt"), 1)

                xnt_mm(0)
                for t in range(NT):
                    sl = t % 2
                    if t >= 1:
                        t_.wait_ge(h("se1v"), t)
                    quarter(KO, sl, ps_b, h("sk4"), t == 0, xnf_t=t)
                    if t >= 1:
                        t_.wait_ge(h("sqpf"), t)
                    quarter(VO, sl, ps_c, h("sv4"), False)
                    t_.wait_ge(h("sks"), t + 1)
                    quarter(UO, sl, ps_b, h("su4"), False)
                    t_.wait_ge(h("sve"), t + 1)
                    quarter(QO, sl, ps_c, h("sq4"), False)
                    t_.wait_ge(h("skp"), t + 1)
                    for hh in range(HN):
                        mm = t_.matmul(s_ps[:, hh * 64:(hh + 1) * 64],
                                       kp_sb[:, sl, hh * 128:(hh + 1) * 128],
                                       v_sb[:, sl, hh * 64:(hh + 1) * 64],
                                       start=(t == 0 and hh % 8 == 0),
                                       stop=(t == NT - 1 and hh % 8 == 7))
                    mm.then_inc(h("ssm"), 1)
                    t_.wait_ge(h("sqp"), t + 1)
                    t_.wait_ge(h("sue"), t + 1)
                    for c in range(8):
                        mm = t_.matmul(ps_b[:, c * 128:(c + 1) * 128],
                                       qp_sb[:, sl, c * 128:(c + 1) * 128], ident[:],
                                       start=True, stop=True)
                    mm.then_inc(h("st1"), 1)
                    t_.wait_ge(h("sqs"), t + 1)
                    for c in range(8):
                        mm = t_.matmul(ps_c[:, c * 128:(c + 1) * 128],
                                       qp_sb[:, sl, 1024 + c * 128:1024 + (c + 1) * 128], ident[:],
                                       start=True, stop=True)
                    mm.then_inc(h("st2"), 1)
                    if t + 1 < NT:
                        xnt_mm(t + 1)

            blk1.__exit__(None, None, None)
            blk1b = nc.Block("b1b")
            blk1b.__enter__()

            @blk1b.vector
            def _(v):
                v.tensor_copy(s_sb[:], s_ps[:]).then_inc(h("ssbv"), 1)

            blk1b.__exit__(None, None, None)

        s2 = contextlib.ExitStack()
        with s2:
            E2 = lambda cm: s2.enter_context(cm)
            w2_bf = E2(nc.sbuf_tensor("w2_bf", [128, 32 * D], BF16))
            b1c_sb = E2(nc.sbuf_tensor("b1c_sb", [128, 32], F32))
            bo_row = E2(nc.sbuf_tensor("bo_row", [1, D], BF16))
            b2_row = E2(nc.sbuf_tensor("b2_row", [1, D], BF16))
            odd1_sb = E2(nc.sbuf_tensor("odd1_sb", [128, 8, 4], BF16))
            even1_sb = E2(nc.sbuf_tensor("even1_sb", [1, 4, D], BF16))
            even2_sb = E2(nc.sbuf_tensor("even2_sb", [16, D], BF16))
            odd2_sb = E2(nc.sbuf_tensor("odd2_sb", [128, 32, 16], BF16))
            qpt = E2(nc.sbuf_tensor("qpt", [128, 2, 2048], BF16))
            ut = E2(nc.sbuf_tensor("ut", [128, 1, D], BF16))
            xt = E2(nc.sbuf_tensor("xt", [128, 2, D], F32))
            o1_sb = E2(nc.sbuf_tensor("o1_sb", [128, D], F32))
            y_sb = E2(nc.sbuf_tensor("y_sb", [128, D], BF16))
            yf_sb = E2(nc.sbuf_tensor("yf_sb", [128, 2, D], BF16))
            x1_g = E2(nc.sbuf_tensor("x1_g", [128, GT, D], F32))
            xn2_tm = E2(nc.sbuf_tensor("xn2_tm", [128, 2, D], BF16))
            xn2_fm = E2(nc.sbuf_tensor("xn2_fm", [128, 8, 512], BF16))
            h_fm = E2(nc.sbuf_tensor("h_fm", [128, 32, 512], BF16))
            d4p = E2(nc.sbuf_tensor("d4p", [4, 512], BF16))
            d4row = E2(nc.sbuf_tensor("d4row", [1, 2048], BF16))
            hd_sb = E2(nc.sbuf_tensor("hd_sb", [16, 512], BF16))
            ps_sm = E2(nc.psum_tensor("ps_sm", [16, 1024], F32))

            blk2 = nc.Block("b2")
            blk2.__enter__()

            @blk2.gpsimd
            def _(g_):
                g_.dma_start(out=b1c_sb[:], in_=b1c_h[:]).then_inc(h("brl"), 16)
                g_.dma_start(out=bo_row[:], in_=bo_h[:]).then_inc(h("brl"), 16)
                g_.dma_start(out=b2_row[:], in_=b2_h[:]).then_inc(h("brl"), 16)
                g_.memset(even2_sb[:], 0.0)
                g_.drain()
                g_.wait_ge(h("sxn2"), 1)
                for i in range(4):
                    g_.dma_start(out=even1_sb[0:1, i, :],
                                 in_=xn2_tm[1 + 2 * i:2 + 2 * i, 0, :]).then_inc(h("se1"), 16)
                for i in range(4):
                    for l in range(4):
                        g_.dma_start(out=even2_sb[4 * i + l:4 * i + l + 1, 256 * i:256 * (i + 1)],
                                     in_=xn2_tm[9 + 2 * i:10 + 2 * i, 0, l:D:4],
                                     allow_slow_non_contiguous=True).then_inc(h("se2"), 16)
                g_.wait_ge(h("ws0"), 32)
                g_.dma_start(out=w1sl[:, 2, :],
                             in_=w1_scr[:, 2 * D:3 * D]).then_inc(h("wl2"), 16)
                g_.wait_ge(h("sd4e"), 1)
                g_.dma_start(out=d4row[0:1, :], in_=d4p[:]).then_inc(h("sd4r"), 16)
                for g in range(NG):
                    if g >= 1:
                        g_.wait_ge(h("sd4e"), g + 1)
                        g_.wait_ge(h("sw1mg"), g)
                        g_.dma_start(out=d4row[0:1, :], in_=d4p[:]).then_inc(h("sd4r"), 16)
                    for fc in range(32):
                        idx = 32 * g + fc
                        if idx < 3 or idx % 3 != 2:
                            continue
                        g_.wait_ge(h("sw1cc"), idx - 2)
                        g_.dma_start(out=w1sl[:, 2, :],
                                     in_=w1_scr[:, fc * D:(fc + 1) * D]).then_inc(h("wl2"), 16)
                g_.wait_ge(h("sd4r"), 16 * NG)
                g_.wait_ge(h("se1"), 64)
                g_.wait_ge(h("se2"), 256)

            @blk2.sync
            def _(sy):
                def qp_load(t):
                    sl = t % 2
                    if t >= 2:
                        sy.wait_ge(h("somm"), t - 1)
                    sy.dma_start(out=qpt[:, sl, :],
                                 in_=qp_scr[t * 128:(t + 1) * 128, :]).then_inc(h("ql%d" % sl), 16)

                def u_load(t):
                    if t >= 1:
                        sy.wait_ge(h("ssy"), t)
                    sy.dma_start(out=ut[:, 0, :],
                                 in_=u_scr[t * 128:(t + 1) * 128, :]).then_inc(h("ul"), 16)

                def x_load(t):
                    sl = t % 2
                    if t >= 2:
                        sy.wait_ge(h("sx1e"), t - 1)
                    sy.dma_start(out=xt[:, sl, :],
                                 in_=x_h[t * 128:(t + 1) * 128, :]).then_inc(h("xa%d" % sl), 16)

                for g in range(NG):
                    if g >= 1:
                        for ti in range(GT):
                            tp = (g - 1) * GT + ti
                            sy.wait_ge(h("sw2e"), tp + 1)
                            sy.dma_start(out=out_h[tp * 128:(tp + 1) * 128, :],
                                         in_=x1_g[:, tp % GT, :]).then_inc(h("sot%d" % (tp % GT)), 16)
                    for ti in range(GT):
                        t = g * GT + ti
                        if g == 0 or ti >= 2:
                            qp_load(t)
                        if g == 0 or ti >= 1:
                            u_load(t)
                            x_load(t)
                    if g + 1 < NG:
                        qp_load(g * GT + GT)
                        qp_load(g * GT + GT + 1)
                        u_load(g * GT + GT)
                        x_load(g * GT + GT)
                    for fc in range(32):
                        idx = 32 * g + fc
                        if idx < 3 or idx % 3 == 2:
                            continue
                        sy.wait_ge(h("sw1cc"), idx - 2)
                        sy.dma_start(out=w1sl[:, idx % 3, :],
                                     in_=w1_scr[:, fc * D:(fc + 1) * D]).then_inc(h("wl%d" % (idx % 3)), 16)
                for ti in range(GT):
                    tp = (NG - 1) * GT + ti
                    sy.wait_ge(h("sw2e"), tp + 1)
                    sy.dma_start(out=out_h[tp * 128:(tp + 1) * 128, :],
                                 in_=x1_g[:, tp % GT, :]).then_inc(h("sot%d" % (tp % GT)), 16)
                for q4 in range(GT):
                    sy.wait_ge(h("sot%d" % q4), 16 * NG)

            @blk2.vector
            def _(v):
                def stats2a(t):
                    sl = t % 2
                    v.wait_ge(h("somm"), t + 1)
                    v.bn_stats(bn6[:, sl, 0:6], ps_a[:, 0:512])
                    v.bn_stats(bn6[:, sl, 6:12], ps_a[:, 512:1024])
                    v.drain()
                    v.bn_aggr(mv_c[:, sl, :], bn6[:, sl, :]).then_inc(h("sbn2"), 1)

                def stats2b(t):
                    sl = t % 2
                    v.wait_ge(h("ssd2"), t + 1)
                    v.reciprocal(inv_c[:, sl:sl + 1], sd_c[:, sl:sl + 1])
                    v.drain()
                    v.tensor_scalar(nmi_c[:, sl:sl + 1], mv_c[:, sl, 0:1],
                                    inv_c[:, sl:sl + 1], -1.0, ALU.mult, ALU.mult).then_inc(h("siv2"), 1)

                def stats2(t):
                    stats2a(t)
                    stats2b(t)

                def ychain(t):
                    v.wait_ge(h("so1"), t + 1)
                    v.wait_ge(h("ul"), 16 * (t + 1))
                    if t >= 1:
                        v.wait_ge(h("syt"), t)
                    v.tensor_mul(o1_sb[:], o1_sb[:], lnw_rep[:])
                    v.drain()
                    v.tensor_add(o1_sb[:], o1_sb[:], lnb_rep[:])
                    v.drain()
                    v.tensor_mul(y_sb[:], o1_sb[:], ut[:, 0, :]).then_inc(h("ssy"), 1)

                def x1ev(t):
                    sl = t % 2
                    ti = t % GT
                    v.wait_ge(h("swom"), t + 1)
                    v.wait_ge(h("xa%d" % sl), 16 * (t // 2 + 1))
                    if t >= GT:
                        v.wait_ge(h("sot%d" % ti), 16 * (t // GT))
                    v.tensor_add(x1_g[:, ti, :], ps_c[:], xt[:, sl, :])
                    v.drain(fusable=True).then_inc(h("sx1e"), 1)

                def ln2(t):
                    sl = t % 2
                    ti = t % GT
                    v.bn_stats(bn6[:, sl, 0:6], x1_g[:, ti, 0:512])
                    v.bn_stats(bn6[:, sl, 6:12], x1_g[:, ti, 512:1024])
                    v.drain()
                    v.bn_aggr(mv_c[:, sl, :], bn6[:, sl, :]).then_inc(h("sbn3"), 1)
                    v.wait_ge(h("ssd3"), t + 1)
                    v.reciprocal(inv_c[:, sl:sl + 1], sd_c[:, sl:sl + 1])
                    v.drain()
                    v.tensor_scalar(nmi_c[:, sl:sl + 1], mv_c[:, sl, 0:1],
                                    inv_c[:, sl:sl + 1], -1.0, ALU.mult, ALU.mult).then_inc(h("siv3"), 1)

                stats2(0)
                ychain(0)
                for g in range(NG):
                    for ti in range(GT):
                        t = g * GT + ti
                        if ti < GT - 1:
                            stats2a(t + 1)
                            stats2b(t + 1)
                            ychain(t + 1)
                            x1ev(t)
                            ln2(t)
                        else:
                            x1ev(t)
                            ln2(t)
                            if t + 1 < NT:
                                stats2a(t + 1)
                                stats2b(t + 1)
                        if t == 1:
                            v.wait_ge(h("sxn2f"), 1)
                            v.drain()
                            for kc in range(8):
                                cp = v.tensor_copy(odd1_sb[:, kc, :], xn2_fm[:, kc, 2:9:2])
                            cp.then_inc(h("sodd"), 1)
                            v.memset(odd2_sb[:], 0.0)
                            v.drain()
                            for kc in range(32):
                                l = kc // 8
                                cp = v.tensor_copy(odd2_sb[:, kc, l:16:4], xn2_fm[:, kc % 8, 10:17:2])
                            cp.then_inc(h("sodd"), 1)
                    if (g + 1) * GT < NT:
                        ychain((g + 1) * GT)
                    for ti in range(GT):
                        t = g * GT + ti
                        v.wait_ge(h("sw2t"), t + 1)
                        v.tensor_add(x1_g[:, ti, :], ps_c[:], x1_g[:, ti, :]).then_inc(h("sw2e"), 1)


            @blk2.scalar
            def _(s):
                s.dma_start(out=w2_bf[:, 0:16 * D], in_=w2_scr[:, 0:16 * D]).then_inc(h("w2l"), 16)
                s.dma_start(out=w2_bf[:, 16 * D:32 * D], in_=w2_scr[:, 16 * D:32 * D]).then_inc(h("w2l"), 16)
                s.wait_ge(h("ssbv"), 1)

                def xn2f_ev(t):
                    s.wait_ge(h("sxn2t"), t + 1)
                    s.activation(
                        xn2_fm[:, :, (t % GT) * 128:(t % GT + 1) * 128],
                        ps_a[:].rearrange("p (k c) -> p k c", k=8),
                        AF.Identity).then_inc(h("sxn2f"), 1)

                def sqrt2_o1(t):
                    sl = t % 2
                    s.wait_ge(h("sbn2"), t + 1)
                    s.activation(sd_c[:, sl:sl + 1], mv_c[:, sl, 1:2], AF.Sqrt,
                                 bias=eps_col[:, 0:1]).then_inc(h("ssd2"), 1)
                    s.wait_ge(h("siv2"), t + 1)
                    if t >= 1:
                        s.wait_ge(h("ssy"), t)
                    s.activation(o1_sb[:], ps_a[:], AF.Identity,
                                 bias=nmi_c[:, sl:sl + 1], scale=inv_c[:, sl:sl + 1]).then_inc(h("so1"), 1)

                def yf_ev(t):
                    sl = t % 2
                    s.wait_ge(h("syt"), t + 1)
                    if t >= 2:
                        s.wait_ge(h("swom"), t - 1)
                    s.activation(yf_sb[:, sl, :], ps_b[:], AF.Identity).then_inc(h("syte"), 1)

                def ln2_tail(t):
                    sl = t % 2
                    ti = t % GT
                    s.wait_ge(h("sbn3"), t + 1)
                    s.activation(sd_c[:, sl:sl + 1], mv_c[:, sl, 1:2], AF.Sqrt,
                                 bias=eps_col[:, 0:1]).then_inc(h("ssd3"), 1)
                    s.wait_ge(h("siv3"), t + 1)
                    if t >= 2:
                        s.wait_ge(h("sxn2t"), t - 1)
                    if t == 2:
                        s.wait_ge(h("se1"), 64)
                        s.wait_ge(h("se2"), 256)
                    s.activation(xn2_tm[:, sl, :], x1_g[:, ti, :], AF.Identity,
                                 bias=nmi_c[:, sl:sl + 1], scale=inv_c[:, sl:sl + 1]).then_inc(h("sxn2"), 1)

                sqrt2_o1(0)
                for g in range(NG):
                    for ti in range(GT):
                        t = g * GT + ti
                        if ti < GT - 1:
                            yf_ev(t)
                            sqrt2_o1(t + 1)
                            if t >= 1 and (t - 1) % GT != GT - 1:
                                xn2f_ev(t - 1)
                            ln2_tail(t)
                        else:
                            yf_ev(t)
                            xn2f_ev(t - 1)
                            ln2_tail(t)
                            xn2f_ev(t)
                            if t + 1 < NT:
                                sqrt2_o1(t + 1)
                    s.wait_ge(h("sd4"), g + 1)
                    if g >= 1:
                        s.wait_ge(h("sd4r"), 16 * g)
                    s.activation(d4p[:], ps_sm[0:4, 0:512], AF.Identity,
                                 scale=scol[0:4, 0:1]).then_inc(h("sd4e"), 1)
                    for fc in range(32):
                        s.wait_ge(h("sw1cc"), 32 * g + fc + 1)
                        s.activation(h_fm[:, fc, :], ps_b[:, (fc % 2) * 512:(fc % 2 + 1) * 512],
                                     AF.Gelu, bias=b1c_sb[:, fc:fc + 1]).then_inc(h("sgel"), 1)
                    s.wait_ge(h("shd"), g + 1)
                    if g >= 1:
                        s.wait_ge(h("sw2t"), 4 * g)
                    s.activation(hd_sb[:], ps_sm[0:16, 512:1024], AF.Identity,
                                 scale=scol[0:16, 0:1]).then_inc(h("shde"), 1)

            @blk2.tensor
            def _(t_):
                t_.wait_ge(h("ssbv"), 1)
                t_.wait_ge(h("wol"), 16)
                t_.wait_ge(h("brl"), 48)

                def o_mm(t, xnf=None):
                    sl = t % 2
                    t_.wait_ge(h("ql%d" % sl), 16 * (t // 2 + 1))
                    if t >= 1:
                        t_.wait_ge(h("so1"), t)
                    if t >= 2:
                        t_.wait_ge(h("sxn2f"), t - 2 if xnf is None else xnf)
                    for hh in range(HN):
                        mm = t_.matmul(ps_a[:, hh * 64:(hh + 1) * 64],
                                       qpt[:, sl, hh * 128:(hh + 1) * 128],
                                       s_sb[:, hh * 64:(hh + 1) * 64],
                                       start=True, stop=True)
                    mm.then_inc(h("somm"), 1)

                o_mm(0)
                for g in range(NG):
                    for ti in range(GT):
                        t = g * GT + ti
                        sl = t % 2
                        if ti < GT - 1 and t + 1 < NT:
                            o_mm(t + 1)
                        t_.wait_ge(h("ssy"), t + 1)
                        if t >= 1:
                            t_.wait_ge(h("syte"), t)
                        if ti == 0 and g >= 1:
                            t_.wait_ge(h("sgel"), 32 * g)
                        for c in range(8):
                            mm = t_.matmul(ps_b[:, c * 128:(c + 1) * 128],
                                           y_sb[:, c * 128:(c + 1) * 128], ident[:],
                                           start=True, stop=True)
                        mm.then_inc(h("syt"), 1)
                        t_.wait_ge(h("syte"), t + 1)
                        if t >= 1:
                            t_.wait_ge(h("sx1e"), t)
                        if ti == 0 and g >= 1:
                            t_.wait_ge(h("sw2e"), 4 * g)
                        for seg in range(2):
                            for kc in range(8):
                                t_.matmul(ps_c[:, seg * 512:(seg + 1) * 512],
                                          yf_sb[:, sl, kc * 128:(kc + 1) * 128],
                                          wo_bf[:, kc * D + seg * 512:kc * D + (seg + 1) * 512],
                                          start=(kc == 0), stop=False)
                            mm = t_.matmul(ps_c[:, seg * 512:(seg + 1) * 512],
                                           ones_row[0:1, 0:128],
                                           bo_row[0:1, seg * 512:(seg + 1) * 512],
                                           start=False, stop=True)
                        mm.then_inc(h("swom"), 1)
                        if t >= 1 and (t - 1) % GT != GT - 1:
                            tm = t - 1
                            t_.wait_ge(h("sxn2"), tm + 1)
                            if ti < GT - 1:
                                t_.wait_ge(h("so1"), t + 2)
                            else:
                                t_.wait_ge(h("sxn2f"), tm)
                            for c in range(8):
                                mm = t_.matmul(ps_a[:, c * 128:(c + 1) * 128],
                                               xn2_tm[:, tm % 2, c * 128:(c + 1) * 128], ident[:],
                                               start=True, stop=True)
                            mm.then_inc(h("sxn2t"), 1)
                    tb = g * GT + GT - 1
                    t_.wait_ge(h("sxn2"), tb + 1)
                    t_.wait_ge(h("sxn2f"), tb)
                    for c in range(8):
                        mm = t_.matmul(ps_a[:, c * 128:(c + 1) * 128],
                                       xn2_tm[:, tb % 2, c * 128:(c + 1) * 128], ident[:],
                                       start=True, stop=True)
                    mm.then_inc(h("sxn2t"), 1)
                    if g + 1 < NG:
                        o_mm((g + 1) * GT, xnf=(g + 1) * GT)
                    t_.wait_ge(h("sxn2f"), 4 * (g + 1))
                    if g == 0:
                        t_.wait_ge(h("sodd"), 1)
                    if g >= 1:
                        t_.wait_ge(h("sd4e"), g)
                    for kc in range(8):
                        mm = t_.matmul(ps_sm[0:4, 0:512], odd1_sb[:, kc, :], xn2_fm[:, kc, :],
                                       start=(kc == 0), stop=(kc == 7))
                    mm.then_inc(h("sd4"), 1)
                    if g == 0:
                        t_.wait_ge(h("se1"), 64)
                    t_.wait_ge(h("syte"), 4 * (g + 1))
                    t_.wait_ge(h("sxn2f"), 4 * (g + 1))
                    for fc in range(32):
                        idx = 32 * g + fc
                        i_fc = fc // 8
                        t_.wait_ge(h("wl%d" % (idx % 3)), 16 * (idx // 3 + 1))
                        if idx >= 2:
                            t_.wait_ge(h("sgel"), idx - 1)
                        psl = ps_b[:, (fc % 2) * 512:(fc % 2 + 1) * 512]
                        for kc in range(8):
                            t_.matmul(psl, w1sl[:, idx % 3, kc * 128:(kc + 1) * 128],
                                      xn2_fm[:, kc, :], start=(kc == 0), stop=False)
                        if fc == 0:
                            t_.wait_ge(h("sd4r"), 16 * (g + 1))
                        mm = t_.matmul(psl, even1_sb[0:1, i_fc, (fc % 8) * 128:(fc % 8 + 1) * 128],
                                       d4row[0:1, i_fc * 512:(i_fc + 1) * 512],
                                       start=False, stop=True)
                        mm.then_inc(h("sw1cc"), 1)
                        if fc == 31:
                            t_.nop().then_inc(h("sw1mg"), 1)
                        if fc == 2 and g == 0:
                            t_.wait_ge(h("sodd"), 2)
                        if fc == 2 and g >= 1:
                            t_.wait_ge(h("shde"), g)
                        if fc >= 2:
                            kch = fc - 2
                            t_.wait_ge(h("sgel"), 32 * g + kch + 1)
                            t_.matmul(ps_sm[0:16, 512:1024], odd2_sb[:, kch, :], h_fm[:, kch, :],
                                      start=(kch == 0), stop=False)
                    if g == 0:
                        t_.wait_ge(h("w2l"), 32)
                        t_.wait_ge(h("se2"), 256)
                    for kch in range(30, 32):
                        t_.wait_ge(h("sgel"), 32 * g + kch + 1)
                        mm = t_.matmul(ps_sm[0:16, 512:1024], odd2_sb[:, kch, :], h_fm[:, kch, :],
                                       start=False, stop=(kch == 31))
                    mm.then_inc(h("shd"), 1)
                    t_.wait_ge(h("shde"), g + 1)
                    for ti in range(GT):
                        t = g * GT + ti
                        if ti == 0:
                            t_.wait_ge(h("sx1e"), 4 * (g + 1))
                        else:
                            t_.wait_ge(h("sw2e"), t)
                        for seg in range(2):
                            for kc in range(32):
                                t_.matmul(ps_c[:, seg * 512:(seg + 1) * 512],
                                          h_fm[:, kc, ti * 128:(ti + 1) * 128],
                                          w2_bf[:, kc * D + seg * 512:kc * D + (seg + 1) * 512],
                                          start=(kc == 0), stop=False)
                            t_.matmul(ps_c[:, seg * 512:(seg + 1) * 512],
                                      ones_row[0:1, 0:128],
                                      b2_row[0:1, seg * 512:(seg + 1) * 512],
                                      start=False, stop=False)
                            mm = t_.matmul(ps_c[:, seg * 512:(seg + 1) * 512],
                                           hd_sb[:, ti * 128:(ti + 1) * 128],
                                           even2_sb[:, seg * 512:(seg + 1) * 512],
                                           start=False, stop=True)
                        mm.then_inc(h("sw2t"), 1)

            blk2.__exit__(None, None, None)

    return nc


def _host_stage(inputs):
    x = np.ascontiguousarray(inputs["L0_attn_x"], dtype=np.float32)
    Wq = np.asarray(inputs["L0_attn_Wqkvu"], dtype=np.float32)
    Wo = np.asarray(inputs["L0_attn_Wout"], dtype=np.float32)
    W1 = np.asarray(inputs["L0_ffn_W1"], dtype=np.float32)
    W2 = np.asarray(inputs["L0_ffn_W2"], dtype=np.float32)

    def pmaj(wt):
        K, F = wt.shape
        nk = K // 128
        return np.ascontiguousarray(wt.reshape(nk, 128, F).transpose(1, 0, 2).reshape(128, nk * F))

    wq = pmaj(np.ascontiguousarray(Wq.T))
    wo = pmaj(np.ascontiguousarray(Wo.T))
    w2 = pmaj(np.ascontiguousarray(W2.T))
    w1 = np.ascontiguousarray(
        W1.reshape(32, 128, 8, 128).transpose(3, 0, 2, 1).reshape(128, 32 * D))
    rep = lambda a: np.ascontiguousarray(
        np.broadcast_to(np.asarray(a, dtype=np.float32).reshape(1, -1), (128, a.size)))
    tc = np.ascontiguousarray(
        np.arange(NT, dtype=np.float32)[None, :] * 128.0
        + np.arange(128, dtype=np.float32)[:, None])
    shared = {
        "wq": wq, "wo": wo, "w1": w1, "w2": w2,
        "bq": np.asarray(inputs["L0_attn_bqkvu"], dtype=np.float32).reshape(1, H4),
        "bo": np.asarray(inputs["L0_attn_bout"], dtype=np.float32).reshape(1, D),
        "b1": np.asarray(inputs["L0_ffn_b1"], dtype=np.float32).reshape(1, H4),
        "b1c": np.ascontiguousarray(
            np.asarray(inputs["L0_ffn_b1"], dtype=np.float32).reshape(32, 128).T),
        "b2": np.asarray(inputs["L0_ffn_b2"], dtype=np.float32).reshape(1, D),
        "lnw": rep(np.asarray(inputs["L0_attn_ln_w"])),
        "lnb": rep(np.asarray(inputs["L0_attn_ln_b"])),
        "th": rep(np.asarray(inputs["L0_attn_theta"]).reshape(-1)),
        "tc": tc,
        "sc": np.ascontiguousarray(np.broadcast_to(
            np.asarray(inputs["L0_ffn_scale"], dtype=np.float32).reshape(1, 1), (128, 1))),
        "idm": np.eye(128, dtype=np.float32),
    }
    shared = {k: np.ascontiguousarray(v, dtype=np.float32) for k, v in shared.items()}
    return [dict(shared, x=np.ascontiguousarray(x[b])) for b in range(8)]


def kernel(**inputs):
    in_maps = _host_stage(inputs)
    nc = build_nc()
    res = run_bass_kernel_spmd(nc, in_maps, core_ids=list(range(8)))
    out = np.stack([np.asarray(r["y"]) for r in res.results], axis=0)
    return out.astype(np.float32)


# version 35 (final)
# speedup vs baseline: 1.0714x; 1.0714x over previous
import contextlib
import numpy as np

import concourse.bass as bass
import concourse.mybir as mybir
from concourse.bass_utils import run_bass_kernel_spmd

F32 = mybir.dt.float32
BF16 = mybir.dt.bfloat16
AF = mybir.ActivationFunctionType
ALU = mybir.AluOpType

D = 1024
HN = 16
NTOK = 2048
NT = 16
GT = 4
NG = NT // GT
H4 = 4096
EPS = 1e-5
PI = float(np.pi)
QO, KO, VO, UO = 0, D, 2 * D, 3 * D


def build_nc():
    nc = bass.Bass("TRN2", target_bir_lowering=False)

    x_h = nc.dram_tensor("x", [NTOK, D], F32, kind="ExternalInput")
    wq_h = nc.dram_tensor("wq", [128, 8 * H4], F32, kind="ExternalInput")
    wo_h = nc.dram_tensor("wo", [128, 8 * D], F32, kind="ExternalInput")
    w1_h = nc.dram_tensor("w1", [128, 32 * D], F32, kind="ExternalInput")
    w2_h = nc.dram_tensor("w2", [128, 32 * D], F32, kind="ExternalInput")
    bq_h = nc.dram_tensor("bq", [1, H4], F32, kind="ExternalInput")
    bo_h = nc.dram_tensor("bo", [1, D], F32, kind="ExternalInput")
    b1_h = nc.dram_tensor("b1", [1, H4], F32, kind="ExternalInput")
    b1c_h = nc.dram_tensor("b1c", [128, 32], F32, kind="ExternalInput")
    b2_h = nc.dram_tensor("b2", [1, D], F32, kind="ExternalInput")
    lnw_h = nc.dram_tensor("lnw", [128, D], F32, kind="ExternalInput")
    lnb_h = nc.dram_tensor("lnb", [128, D], F32, kind="ExternalInput")
    th_h = nc.dram_tensor("th", [128, D], F32, kind="ExternalInput")
    tc_h = nc.dram_tensor("tc", [128, NT], F32, kind="ExternalInput")
    sc_h = nc.dram_tensor("sc", [128, 1], F32, kind="ExternalInput")
    id_h = nc.dram_tensor("idm", [128, 128], F32, kind="ExternalInput")
    out_h = nc.dram_tensor("y", [NTOK, D], F32, kind="ExternalOutput")
    qp_scr = nc.dram_tensor("qp_scr", [NTOK, 2048], BF16, kind="Internal")
    u_scr = nc.dram_tensor("u_scr", [NTOK, D], BF16, kind="Internal")
    w1_scr = nc.dram_tensor("w1_scr", [128, 32 * D], BF16, kind="Internal")
    w2_scr = nc.dram_tensor("w2_scr", [128, 32 * D], BF16, kind="Internal")

    es = contextlib.ExitStack()
    with es:
        E = lambda cm: es.enter_context(cm)
        ident = E(nc.sbuf_tensor("ident", [128, 128], BF16))
        scol = E(nc.sbuf_tensor("scol", [128, 1], F32))
        lnw_rep = E(nc.sbuf_tensor("lnw_rep", [128, D], BF16))
        lnb_rep = E(nc.sbuf_tensor("lnb_rep", [128, D], BF16))
        ones_row = E(nc.sbuf_tensor("ones_row", [1, 512], BF16))
        pi_col = E(nc.sbuf_tensor("pi_col", [128, 1], F32))
        p32_col = E(nc.sbuf_tensor("p32_col", [128, 1], F32))
        npi_col = E(nc.sbuf_tensor("npi_col", [128, 1], F32))
        eps_col = E(nc.sbuf_tensor("eps_col", [128, 1], F32))
        wo_bf = E(nc.sbuf_tensor("wo_bf", [128, 8 * D], BF16))
        w1sl = E(nc.sbuf_tensor("w1sl", [128, 3, D], BF16))
        s_sb = E(nc.sbuf_tensor("s_sb", [128, D], BF16))
        mv_c = E(nc.sbuf_tensor("mv_c", [128, 2, 2], F32))
        sd_c = E(nc.sbuf_tensor("sd_c", [128, 2], F32))
        inv_c = E(nc.sbuf_tensor("inv_c", [128, 2], F32))
        nmi_c = E(nc.sbuf_tensor("nmi_c", [128, 2], F32))
        bn6 = E(nc.sbuf_tensor("bn6", [128, 2, 12], F32))
        ps_a = E(nc.psum_tensor("ps_a", [128, D], F32))
        ps_b = E(nc.psum_tensor("ps_b", [128, D], F32))
        ps_c = E(nc.psum_tensor("ps_c", [128, D], F32))

        sems = {}
        for n in ["pro", "proc", "wql", "wol",
                  "xl", "sbn", "ssd", "siv", "sxn", "sxnt", "sxnf",
                  "sk4", "sv4", "su4", "sq4", "sks", "sqs",
                  "stbl", "smod", "ssc", "skp", "sqp",
                  "sve", "sue", "ssm", "st1", "st2", "se1v", "sqpf",
                  "sust", "sqst", "w1c", "w1s",
                  "w2l", "ssbv", "ql", "ul", "xl2",
                  "somm", "sbn2", "ssd2", "siv2", "so1", "ssy",
                  "wq0", "wq1", "wq2", "wq3",
                  "syt", "syte", "swom", "sx1e",
                  "sbn3", "ssd3", "siv3", "sxn2", "sxn2t", "sxn2f",
                  "se1", "se2", "sodd", "sd4", "sd4e", "sd4r",
                  "sw1mg", "sw1ld", "sw1cc", "sgel", "shd", "shde",
                  "sw2t", "sw2e", "sout", "brl",
                  "xl0", "xl1", "su0", "su1", "sq0", "sq1", "ws0", "ws1",
                  "ql0", "ql1", "xa0", "xa1", "wl0", "wl1", "wl2",
                  "sot0", "sot1", "sot2", "sot3"]:
            sems[n] = E(nc.semaphore("s_" + n))
        h = lambda n: sems[n]

        PRO_TOT = 16 * 7

        s1 = contextlib.ExitStack()
        with s1:
            E1 = lambda cm: s1.enter_context(cm)
            wq_bf = E1(nc.sbuf_tensor("wq_bf", [128, 8 * H4], BF16))
            bq_row = E1(nc.sbuf_tensor("bq_row", [1, H4], BF16))
            th_rep = E1(nc.sbuf_tensor("th_rep", [128, D], F32))
            tcol = E1(nc.sbuf_tensor("tcol", [128, NT], F32))
            xbuf = E1(nc.sbuf_tensor("xbuf", [128, 2, D], F32))
            xn_tm = E1(nc.sbuf_tensor("xn_tm", [128, 2, D], BF16))
            xn_fm = E1(nc.sbuf_tensor("xn_fm", [128, 2, D], BF16))
            ks_sb = E1(nc.sbuf_tensor("ks_sb", [128, 2, D], BF16))
            qs_sb = E1(nc.sbuf_tensor("qs_sb", [128, 2, D], BF16))
            v_sb = E1(nc.sbuf_tensor("v_sb", [128, 2, D], BF16))
            u_sb = E1(nc.sbuf_tensor("u_sb", [128, 2, D], BF16))
            ang_sb = E1(nc.sbuf_tensor("ang_sb", [128, 2, 2, D], F32))
            dlt_sb = E1(nc.sbuf_tensor("dlt_sb", [128, D], F32))
            ki_sb = E1(nc.sbuf_tensor("ki_sb", [128, D], mybir.dt.int32))
            cos_sb = E1(nc.sbuf_tensor("cos_sb", [128, 2, D], BF16))
            sin_sb = E1(nc.sbuf_tensor("sin_sb", [128, 2, D], BF16))
            kp_sb = E1(nc.sbuf_tensor("kp_sb", [128, 2, 2048], BF16))
            qp_sb = E1(nc.sbuf_tensor("qp_sb", [128, 2, 2048], BF16))
            qf_sb = E1(nc.sbuf_tensor("qf_sb", [128, 2, 2048], BF16))
            w1st = E1(nc.sbuf_tensor("w1st", [128, 2, D], BF16))
            s_ps = E1(nc.psum_tensor("s_ps", [128, D], F32))

            hview = lambda ap: ap.rearrange("p (hh e) -> p hh e", hh=HN)

            blk1 = nc.Block("b1")
            blk1.__enter__()

            @blk1.gpsimd
            def _(g):
                g.memset(ones_row[:], 1.0)
                g.memset(pi_col[:], 2 * PI)
                g.memset(p32_col[:], 0.5 * PI)
                g.memset(npi_col[:], -2 * PI)
                g.memset(eps_col[:], EPS).then_inc(h("proc"), 1)
                g.dma_start(out=th_rep[:], in_=th_h[:]).then_inc(h("pro"), 16)
                g.dma_start(out=tcol[:], in_=tc_h[:]).then_inc(h("pro"), 16)
                g.dma_start(out=scol[:], in_=sc_h[:]).then_inc(h("pro"), 16)
                g.dma_start(out=ident[:], in_=id_h[:]).then_inc(h("pro"), 16)
                g.dma_start(out=lnw_rep[:], in_=lnw_h[:]).then_inc(h("pro"), 16)
                g.dma_start(out=lnb_rep[:], in_=lnb_h[:]).then_inc(h("pro"), 16)
                g.dma_start(out=bq_row[:], in_=bq_h[:]).then_inc(h("pro"), 16)
                for wchunk in range(4):
                    g.dma_start(out=wq_bf[:, wchunk * 2 * H4:(wchunk + 1) * 2 * H4],
                                in_=wq_h[:, wchunk * 2 * H4:(wchunk + 1) * 2 * H4]).then_inc(h("wq%d" % wchunk), 16)
                g.dma_start(out=wo_bf[:], in_=wo_h[:]).then_inc(h("wol"), 16)
                for fc in range(64):
                    sl = fc % 2
                    src = w1_h if fc < 32 else w2_h
                    dst = w1_scr if fc < 32 else w2_scr
                    c = fc % 32
                    if fc >= 2:
                        g.wait_ge(h("ws%d" % sl), 16 * (fc // 2))
                    g.dma_start(out=w1st[:, sl, :], in_=src[:, c * D:(c + 1) * D]).then_inc(h("w1c"), 16)
                    g.wait_ge(h("w1c"), 16 * (fc + 1))
                    g.dma_start(out=dst[:, c * D:(c + 1) * D], in_=w1st[:, sl, :]).then_inc(h("ws%d" % sl), 16)
                g.wait_ge(h("ws0"), 16 * 32)
                g.wait_ge(h("ws1"), 16 * 32)

            @blk1.sync
            def _(sy):
                for t in range(NT + 1):
                    if t < NT:
                        if t >= 2:
                            sy.wait_ge(h("sxn"), t - 1)
                        sy.dma_start(out=xbuf[:, t % 2, :],
                                     in_=x_h[t * 128:(t + 1) * 128, :]).then_inc(h("xl%d" % (t % 2)), 16)
                    if t >= 1:
                        tp = t - 1
                        sy.wait_ge(h("sue"), tp + 1)
                        sy.dma_start(out=u_scr[tp * 128:(tp + 1) * 128, :],
                                     in_=u_sb[:, tp % 2, :]).then_inc(h("su%d" % (tp % 2)), 16)
                        sy.wait_ge(h("sqpf"), tp + 1)
                        sy.dma_start(out=qp_scr[tp * 128:(tp + 1) * 128, :],
                                     in_=qf_sb[:, tp % 2, :]).then_inc(h("sq%d" % (tp % 2)), 16)
                sy.wait_ge(h("ws0"), 16)
                sy.dma_start(out=w1sl[:, 0, :], in_=w1_scr[:, 0:D]).then_inc(h("wl0"), 16)
                sy.wait_ge(h("ws1"), 16)
                sy.dma_start(out=w1sl[:, 1, :], in_=w1_scr[:, D:2 * D]).then_inc(h("wl1"), 16)
                sy.wait_ge(h("su0"), 16 * 8)
                sy.wait_ge(h("su1"), 16 * 8)
                sy.wait_ge(h("sq0"), 16 * 8)
                sy.wait_ge(h("sq1"), 16 * 8)

            @blk1.vector
            def _(v):
                v.wait_ge(h("pro"), PRO_TOT)
                v.wait_ge(h("proc"), 1)
                v.tensor_scalar_mul(dlt_sb[:], th_rep[:], 6.4206382432985265e-09)
                v.drain()
                v.scalar_tensor_tensor(th_rep[:], th_rep[:], 0.15915493667125702,
                                       dlt_sb[:], ALU.mult, ALU.add).then_inc(h("smod"), 1)

                def prep_a(t):
                    sl = t % 2
                    v.wait_ge(h("xl%d" % sl), 16 * (t // 2 + 1))
                    v.bn_stats(bn6[:, sl, 0:6], xbuf[:, sl, 0:512])
                    v.bn_stats(bn6[:, sl, 6:12], xbuf[:, sl, 512:1024])
                    v.drain()
                    v.bn_aggr(mv_c[:, sl, :], bn6[:, sl, :]).then_inc(h("sbn"), 1)

                def prep_b(t):
                    sl = t % 2
                    v.wait_ge(h("ssd"), t + 1)
                    v.reciprocal(inv_c[:, sl:sl + 1], sd_c[:, sl:sl + 1])
                    v.drain()
                    v.tensor_scalar(nmi_c[:, sl:sl + 1], mv_c[:, sl, 0:1],
                                    inv_c[:, sl:sl + 1], -1.0, ALU.mult, ALU.mult).then_inc(h("siv"), 1)

                def prep(t):
                    prep_a(t)
                    prep_b(t)

                def wrap_cos(sl):
                    v.scalar_tensor_tensor(ang_sb[:, sl, 1, :], ang_sb[:, sl, 0, :], 0.5,
                                           ang_sb[:, sl, 0, :], ALU.is_ge, ALU.subtract)
                    v.drain()
                    v.scalar_tensor_tensor(ang_sb[:, sl, 0, :], ang_sb[:, sl, 1, :], 0.5,
                                           ang_sb[:, sl, 1, :], ALU.is_ge, ALU.subtract)
                    v.drain()
                    v.scalar_tensor_tensor(ang_sb[:, sl, 1, :], ang_sb[:, sl, 0, :], 0.25,
                                           ang_sb[:, sl, 0, :], ALU.is_ge, ALU.subtract)
                    v.drain(fusable=True).then_inc(h("smod"), 1)

                def chain0():
                    v.wait_ge(h("stbl"), 1)
                    v.tensor_copy(ki_sb[:], ang_sb[:, 0, 0, :])
                    v.drain()
                    v.tensor_copy(ang_sb[:, 0, 1, :], ki_sb[:])
                    v.drain()
                    v.scalar_tensor_tensor(ang_sb[:, 0, 0, :], ang_sb[:, 0, 1, :], -1.0,
                                           ang_sb[:, 0, 0, :], ALU.mult, ALU.add)
                    v.drain()
                    wrap_cos(0)
                    v.tensor_scalar_mul(ang_sb[:, 1, 0, :], th_rep[:], 128.0)
                    v.drain()
                    v.tensor_copy(ki_sb[:], ang_sb[:, 1, 0, :])
                    v.drain()
                    v.tensor_copy(dlt_sb[:], ki_sb[:])
                    v.drain()
                    v.scalar_tensor_tensor(dlt_sb[:], dlt_sb[:], -1.0,
                                           ang_sb[:, 1, 0, :], ALU.mult, ALU.add)
                    v.drain()
                    v.scalar_tensor_tensor(ang_sb[:, 1, 0, :], dlt_sb[:], 0.5,
                                           dlt_sb[:], ALU.is_ge, ALU.subtract)
                    v.drain()
                    v.scalar_tensor_tensor(dlt_sb[:], ang_sb[:, 1, 0, :], 0.5,
                                           ang_sb[:, 1, 0, :], ALU.is_ge, ALU.subtract)
                    v.drain()

                def chain(t):
                    sl = t % 2
                    if t >= 2:
                        v.wait_ge(h("ssc"), t - 1)
                    v.tensor_add(ang_sb[:, sl, 0, :], ang_sb[:, 1 - sl, 0, :], dlt_sb[:])
                    v.drain()
                    wrap_cos(sl)

                prep(0)
                chain0()
                for t in range(NT):
                    sl = t % 2
                    if t + 1 < NT:
                        prep_a(t + 1)
                    v.wait_ge(h("sks"), t + 1)
                    v.wait_ge(h("ssc"), t + 1)
                    if t >= 2:
                        v.wait_ge(h("ssm"), t - 1)
                    v.tensor_mul(hview(kp_sb[:, sl, :])[:, :, 0:64],
                                 hview(ks_sb[:, sl, :]), hview(cos_sb[:, sl, :]))
                    v.tensor_mul(hview(kp_sb[:, sl, :])[:, :, 64:128],
                                 hview(ks_sb[:, sl, :]), hview(sin_sb[:, sl, :])).then_inc(h("skp"), 1)
                    v.wait_ge(h("sv4"), t + 1)
                    v.tensor_copy(v_sb[:, sl, :], ps_c[:]).then_inc(h("sve"), 1)
                    v.wait_ge(h("su4"), t + 1)
                    if t >= 2:
                        v.wait_ge(h("su%d" % sl), 16 * (t // 2))
                    v.tensor_copy(u_sb[:, sl, :], ps_b[:]).then_inc(h("sue"), 1)
                    v.wait_ge(h("sqs"), t + 1)
                    if t >= 2:
                        v.wait_ge(h("st2"), t - 1)
                    v.tensor_mul(hview(qp_sb[:, sl, :])[:, :, 0:64],
                                 hview(qs_sb[:, sl, :]), hview(cos_sb[:, sl, :]))
                    v.tensor_mul(hview(qp_sb[:, sl, :])[:, :, 64:128],
                                 hview(qs_sb[:, sl, :]), hview(sin_sb[:, sl, :])).then_inc(h("sqp"), 1)
                    if t + 1 < NT:
                        prep_b(t + 1)
                    v.wait_ge(h("st1"), t + 1)
                    if t >= 2:
                        v.wait_ge(h("sq%d" % sl), 16 * (t // 2))
                    v.tensor_copy(qf_sb[:, sl, 0:1024], ps_b[:]).then_inc(h("se1v"), 1)
                    v.wait_ge(h("st2"), t + 1)
                    v.tensor_copy(qf_sb[:, sl, 1024:2048], ps_c[:]).then_inc(h("sqpf"), 1)
                    if t + 1 < NT:
                        chain(t + 1)

            @blk1.scalar
            def _(s):
                s.wait_ge(h("pro"), PRO_TOT)
                s.wait_ge(h("proc"), 1)
                s.wait_ge(h("smod"), 1)
                s.activation(ang_sb[:, 0, 0, :], th_rep[:], AF.Identity,
                             scale=tcol[:, 0:1]).then_inc(h("stbl"), 1)
                s.wait_ge(h("sbn"), 1)
                s.activation(sd_c[:, 0:1], mv_c[:, 0, 1:2], AF.Sqrt,
                             bias=eps_col[:, 0:1]).then_inc(h("ssd"), 1)
                for t in range(NT):
                    sl = t % 2
                    s.wait_ge(h("siv"), t + 1)
                    s.wait_ge(h("xl%d" % sl), 16 * (t // 2 + 1))
                    if t >= 2:
                        s.wait_ge(h("sxnt"), t - 1)
                    s.activation(xn_tm[:, sl, :], xbuf[:, sl, :], AF.Identity,
                                 bias=nmi_c[:, sl:sl + 1], scale=inv_c[:, sl:sl + 1]).then_inc(h("sxn"), 1)
                    s.wait_ge(h("sxnt"), t + 1)
                    if t >= 2:
                        s.wait_ge(h("sq4"), t - 1)
                    s.activation(xn_fm[:, sl, 0:512], ps_a[:, 0:512],
                                 AF.Identity).then_inc(h("sxnf"), 1)
                    s.activation(xn_fm[:, sl, 512:1024], ps_a[:, 512:1024],
                                 AF.Identity).then_inc(h("sxnf"), 1)
                    s.wait_ge(h("smod"), t + 2)
                    s.activation(sin_sb[:, sl, :], ang_sb[:, sl, 0, :], AF.Sin, scale=pi_col[:, 0:1])
                    s.activation(cos_sb[:, sl, :], ang_sb[:, sl, 1, :], AF.Sin,
                                 bias=p32_col[:, 0:1], scale=npi_col[:, 0:1]).then_inc(h("ssc"), 1)
                    s.wait_ge(h("sk4"), t + 1)
                    if t >= 2:
                        s.wait_ge(h("skp"), t - 1)
                    s.activation(ks_sb[:, sl, :], ps_b[:], AF.Silu).then_inc(h("sks"), 1)
                    s.wait_ge(h("sq4"), t + 1)
                    if t >= 2:
                        s.wait_ge(h("sqp"), t - 1)
                    s.activation(qs_sb[:, sl, :], ps_c[:], AF.Silu).then_inc(h("sqs"), 1)
                    if t + 1 < NT:
                        slp = (t + 1) % 2
                        s.wait_ge(h("sbn"), t + 2)
                        s.activation(sd_c[:, slp:slp + 1], mv_c[:, slp, 1:2], AF.Sqrt,
                                     bias=eps_col[:, 0:1]).then_inc(h("ssd"), 1)

            @blk1.tensor
            def _(t_):
                t_.wait_ge(h("pro"), PRO_TOT)
                t_.wait_ge(h("proc"), 1)

                def quarter(qoff, sl, psd, sem, first, xnf_t=None):
                    for seg in range(2):
                        for kc in range(8):
                            if first and seg == 0:
                                t_.wait_ge(h("wq%d" % (kc // 2)), 16)
                            if xnf_t is not None and seg == 0 and kc in (0, 4):
                                t_.wait_ge(h("sxnf"), 2 * xnf_t + 1 + (kc // 4))
                            t_.matmul(psd[:, seg * 512:(seg + 1) * 512],
                                      xn_fm[:, sl, kc * 128:(kc + 1) * 128],
                                      wq_bf[:, kc * H4 + qoff + seg * 512:
                                            kc * H4 + qoff + (seg + 1) * 512],
                                      start=(kc == 0), stop=False)
                        mm = t_.matmul(psd[:, seg * 512:(seg + 1) * 512],
                                       ones_row[0:1, 0:128],
                                       bq_row[0:1, qoff + seg * 512:qoff + (seg + 1) * 512],
                                       start=False, stop=True)
                    mm.then_inc(sem, 1)

                for t in range(NT):
                    sl = t % 2
                    t_.wait_ge(h("sxn"), t + 1)
                    if t >= 1:
                        t_.wait_ge(h("sxnf"), 2 * t)
                    for c in range(8):
                        mm = t_.matmul(ps_a[:, c * 128:(c + 1) * 128],
                                       xn_tm[:, sl, c * 128:(c + 1) * 128], ident[:],
                                       start=True, stop=True)
                    mm.then_inc(h("sxnt"), 1)
                    if t >= 1:
                        t_.wait_ge(h("se1v"), t)
                    quarter(KO, sl, ps_b, h("sk4"), t == 0, xnf_t=t)
                    if t >= 1:
                        t_.wait_ge(h("sqpf"), t)
                    quarter(VO, sl, ps_c, h("sv4"), False)
                    t_.wait_ge(h("sks"), t + 1)
                    quarter(UO, sl, ps_b, h("su4"), False)
                    t_.wait_ge(h("sve"), t + 1)
                    quarter(QO, sl, ps_c, h("sq4"), False)
                    t_.wait_ge(h("skp"), t + 1)
                    for hh in range(HN):
                        mm = t_.matmul(s_ps[:, hh * 64:(hh + 1) * 64],
                                       kp_sb[:, sl, hh * 128:(hh + 1) * 128],
                                       v_sb[:, sl, hh * 64:(hh + 1) * 64],
                                       start=(t == 0 and hh % 8 == 0),
                                       stop=(t == NT - 1 and hh % 8 == 7))
                    mm.then_inc(h("ssm"), 1)
                    t_.wait_ge(h("sqp"), t + 1)
                    t_.wait_ge(h("sue"), t + 1)
                    for c in range(8):
                        mm = t_.matmul(ps_b[:, c * 128:(c + 1) * 128],
                                       qp_sb[:, sl, c * 128:(c + 1) * 128], ident[:],
                                       start=True, stop=True)
                    mm.then_inc(h("st1"), 1)
                    t_.wait_ge(h("sqs"), t + 1)
                    for c in range(8):
                        mm = t_.matmul(ps_c[:, c * 128:(c + 1) * 128],
                                       qp_sb[:, sl, 1024 + c * 128:1024 + (c + 1) * 128], ident[:],
                                       start=True, stop=True)
                    mm.then_inc(h("st2"), 1)

            blk1.__exit__(None, None, None)
            blk1b = nc.Block("b1b")
            blk1b.__enter__()

            @blk1b.vector
            def _(v):
                v.tensor_copy(s_sb[:], s_ps[:]).then_inc(h("ssbv"), 1)

            blk1b.__exit__(None, None, None)

        s2 = contextlib.ExitStack()
        with s2:
            E2 = lambda cm: s2.enter_context(cm)
            w2_bf = E2(nc.sbuf_tensor("w2_bf", [128, 32 * D], BF16))
            b1c_sb = E2(nc.sbuf_tensor("b1c_sb", [128, 32], F32))
            bo_row = E2(nc.sbuf_tensor("bo_row", [1, D], BF16))
            b2_row = E2(nc.sbuf_tensor("b2_row", [1, D], BF16))
            odd1_sb = E2(nc.sbuf_tensor("odd1_sb", [128, 8, 4], BF16))
            even1_sb = E2(nc.sbuf_tensor("even1_sb", [1, 4, D], BF16))
            even2_sb = E2(nc.sbuf_tensor("even2_sb", [16, D], BF16))
            odd2_sb = E2(nc.sbuf_tensor("odd2_sb", [128, 32, 16], BF16))
            qpt = E2(nc.sbuf_tensor("qpt", [128, 2, 2048], BF16))
            ut = E2(nc.sbuf_tensor("ut", [128, 1, D], BF16))
            xt = E2(nc.sbuf_tensor("xt", [128, 2, D], F32))
            o1_sb = E2(nc.sbuf_tensor("o1_sb", [128, D], F32))
            y_sb = E2(nc.sbuf_tensor("y_sb", [128, D], BF16))
            yf_sb = E2(nc.sbuf_tensor("yf_sb", [128, 2, D], BF16))
            x1_g = E2(nc.sbuf_tensor("x1_g", [128, GT, D], F32))
            xn2_tm = E2(nc.sbuf_tensor("xn2_tm", [128, 2, D], BF16))
            xn2_fm = E2(nc.sbuf_tensor("xn2_fm", [128, 8, 512], BF16))
            h_fm = E2(nc.sbuf_tensor("h_fm", [128, 32, 512], BF16))
            d4p = E2(nc.sbuf_tensor("d4p", [4, 512], BF16))
            d4row = E2(nc.sbuf_tensor("d4row", [1, 2048], BF16))
            hd_sb = E2(nc.sbuf_tensor("hd_sb", [16, 512], BF16))
            ps_sm = E2(nc.psum_tensor("ps_sm", [16, 1024], F32))

            blk2 = nc.Block("b2")
            blk2.__enter__()

            @blk2.gpsimd
            def _(g_):
                g_.dma_start(out=b1c_sb[:], in_=b1c_h[:]).then_inc(h("brl"), 16)
                g_.dma_start(out=bo_row[:], in_=bo_h[:]).then_inc(h("brl"), 16)
                g_.dma_start(out=b2_row[:], in_=b2_h[:]).then_inc(h("brl"), 16)
                g_.memset(even2_sb[:], 0.0)
                g_.drain()
                g_.wait_ge(h("sxn2"), 1)
                for i in range(4):
                    g_.dma_start(out=even1_sb[0:1, i, :],
                                 in_=xn2_tm[1 + 2 * i:2 + 2 * i, 0, :]).then_inc(h("se1"), 16)
                for i in range(4):
                    for l in range(4):
                        g_.dma_start(out=even2_sb[4 * i + l:4 * i + l + 1, 256 * i:256 * (i + 1)],
                                     in_=xn2_tm[9 + 2 * i:10 + 2 * i, 0, l:D:4],
                                     allow_slow_non_contiguous=True).then_inc(h("se2"), 16)
                g_.wait_ge(h("ws0"), 32)
                g_.dma_start(out=w1sl[:, 2, :],
                             in_=w1_scr[:, 2 * D:3 * D]).then_inc(h("wl2"), 16)
                g_.wait_ge(h("sd4e"), 1)
                g_.dma_start(out=d4row[0:1, :], in_=d4p[:]).then_inc(h("sd4r"), 16)
                for g in range(NG):
                    if g >= 1:
                        g_.wait_ge(h("sd4e"), g + 1)
                        g_.wait_ge(h("sw1mg"), g)
                        g_.dma_start(out=d4row[0:1, :], in_=d4p[:]).then_inc(h("sd4r"), 16)
                    for fc in range(32):
                        idx = 32 * g + fc
                        if idx < 3 or idx % 3 != 2:
                            continue
                        g_.wait_ge(h("sw1cc"), idx - 2)
                        g_.dma_start(out=w1sl[:, 2, :],
                                     in_=w1_scr[:, fc * D:(fc + 1) * D]).then_inc(h("wl2"), 16)
                g_.wait_ge(h("sd4r"), 16 * NG)
                g_.wait_ge(h("se1"), 64)
                g_.wait_ge(h("se2"), 256)

            @blk2.sync
            def _(sy):
                def qp_load(t):
                    sl = t % 2
                    if t >= 2:
                        sy.wait_ge(h("somm"), t - 1)
                    sy.dma_start(out=qpt[:, sl, :],
                                 in_=qp_scr[t * 128:(t + 1) * 128, :]).then_inc(h("ql%d" % sl), 16)

                def u_load(t):
                    if t >= 1:
                        sy.wait_ge(h("ssy"), t)
                    sy.dma_start(out=ut[:, 0, :],
                                 in_=u_scr[t * 128:(t + 1) * 128, :]).then_inc(h("ul"), 16)

                def x_load(t):
                    sl = t % 2
                    if t >= 2:
                        sy.wait_ge(h("sx1e"), t - 1)
                    sy.dma_start(out=xt[:, sl, :],
                                 in_=x_h[t * 128:(t + 1) * 128, :]).then_inc(h("xa%d" % sl), 16)

                for g in range(NG):
                    if g >= 1:
                        for ti in range(GT):
                            tp = (g - 1) * GT + ti
                            sy.wait_ge(h("sw2e"), tp + 1)
                            sy.dma_start(out=out_h[tp * 128:(tp + 1) * 128, :],
                                         in_=x1_g[:, tp % GT, :]).then_inc(h("sot%d" % (tp % GT)), 16)
                    for ti in range(GT):
                        t = g * GT + ti
                        if g == 0 or ti >= 2:
                            qp_load(t)
                        if g == 0 or ti >= 1:
                            u_load(t)
                            x_load(t)
                    if g + 1 < NG:
                        qp_load(g * GT + GT)
                        qp_load(g * GT + GT + 1)
                        u_load(g * GT + GT)
                        x_load(g * GT + GT)
                    for fc in range(32):
                        idx = 32 * g + fc
                        if idx < 3 or idx % 3 == 2:
                            continue
                        sy.wait_ge(h("sw1cc"), idx - 2)
                        sy.dma_start(out=w1sl[:, idx % 3, :],
                                     in_=w1_scr[:, fc * D:(fc + 1) * D]).then_inc(h("wl%d" % (idx % 3)), 16)
                for ti in range(GT):
                    tp = (NG - 1) * GT + ti
                    sy.wait_ge(h("sw2e"), tp + 1)
                    sy.dma_start(out=out_h[tp * 128:(tp + 1) * 128, :],
                                 in_=x1_g[:, tp % GT, :]).then_inc(h("sot%d" % (tp % GT)), 16)
                for q4 in range(GT):
                    sy.wait_ge(h("sot%d" % q4), 16 * NG)

            @blk2.vector
            def _(v):
                def stats2a(t):
                    sl = t % 2
                    v.wait_ge(h("somm"), t + 1)
                    v.bn_stats(bn6[:, sl, 0:6], ps_a[:, 0:512])
                    v.bn_stats(bn6[:, sl, 6:12], ps_a[:, 512:1024])
                    v.drain()
                    v.bn_aggr(mv_c[:, sl, :], bn6[:, sl, :]).then_inc(h("sbn2"), 1)

                def stats2b(t):
                    sl = t % 2
                    v.wait_ge(h("ssd2"), t + 1)
                    v.reciprocal(inv_c[:, sl:sl + 1], sd_c[:, sl:sl + 1])
                    v.drain()
                    v.tensor_scalar(nmi_c[:, sl:sl + 1], mv_c[:, sl, 0:1],
                                    inv_c[:, sl:sl + 1], -1.0, ALU.mult, ALU.mult).then_inc(h("siv2"), 1)

                def stats2(t):
                    stats2a(t)
                    stats2b(t)

                def ychain(t):
                    v.wait_ge(h("so1"), t + 1)
                    v.wait_ge(h("ul"), 16 * (t + 1))
                    if t >= 1:
                        v.wait_ge(h("syt"), t)
                    v.tensor_mul(o1_sb[:], o1_sb[:], lnw_rep[:])
                    v.drain()
                    v.tensor_add(o1_sb[:], o1_sb[:], lnb_rep[:])
                    v.drain()
                    v.tensor_mul(y_sb[:], o1_sb[:], ut[:, 0, :]).then_inc(h("ssy"), 1)

                def x1ev(t):
                    sl = t % 2
                    ti = t % GT
                    v.wait_ge(h("swom"), t + 1)
                    v.wait_ge(h("xa%d" % sl), 16 * (t // 2 + 1))
                    if t >= GT:
                        v.wait_ge(h("sot%d" % ti), 16 * (t // GT))
                    v.tensor_add(x1_g[:, ti, :], ps_c[:], xt[:, sl, :])
                    v.drain(fusable=True).then_inc(h("sx1e"), 1)

                def ln2(t):
                    sl = t % 2
                    ti = t % GT
                    v.bn_stats(bn6[:, sl, 0:6], x1_g[:, ti, 0:512])
                    v.bn_stats(bn6[:, sl, 6:12], x1_g[:, ti, 512:1024])
                    v.drain()
                    v.bn_aggr(mv_c[:, sl, :], bn6[:, sl, :]).then_inc(h("sbn3"), 1)
                    v.wait_ge(h("ssd3"), t + 1)
                    v.reciprocal(inv_c[:, sl:sl + 1], sd_c[:, sl:sl + 1])
                    v.drain()
                    v.tensor_scalar(nmi_c[:, sl:sl + 1], mv_c[:, sl, 0:1],
                                    inv_c[:, sl:sl + 1], -1.0, ALU.mult, ALU.mult).then_inc(h("siv3"), 1)

                stats2(0)
                ychain(0)
                for g in range(NG):
                    for ti in range(GT):
                        t = g * GT + ti
                        if ti < GT - 1:
                            stats2a(t + 1)
                            stats2b(t + 1)
                            ychain(t + 1)
                            x1ev(t)
                            ln2(t)
                        else:
                            x1ev(t)
                            ln2(t)
                            if t + 1 < NT:
                                stats2a(t + 1)
                                stats2b(t + 1)
                        if t == 1:
                            v.wait_ge(h("sxn2f"), 1)
                            v.drain()
                            for kc in range(8):
                                cp = v.tensor_copy(odd1_sb[:, kc, :], xn2_fm[:, kc, 2:9:2])
                            cp.then_inc(h("sodd"), 1)
                            v.memset(odd2_sb[:], 0.0)
                            v.drain()
                            for kc in range(32):
                                l = kc // 8
                                cp = v.tensor_copy(odd2_sb[:, kc, l:16:4], xn2_fm[:, kc % 8, 10:17:2])
                            cp.then_inc(h("sodd"), 1)
                    if (g + 1) * GT < NT:
                        ychain((g + 1) * GT)
                    for ti in range(GT):
                        t = g * GT + ti
                        psw = ps_c if ti % 2 == 0 else ps_a
                        v.wait_ge(h("sw2t"), t + 1)
                        v.tensor_add(x1_g[:, ti, :], psw[:], x1_g[:, ti, :]).then_inc(h("sw2e"), 1)


            @blk2.scalar
            def _(s):
                s.dma_start(out=w2_bf[:, 0:16 * D], in_=w2_scr[:, 0:16 * D]).then_inc(h("w2l"), 16)
                s.dma_start(out=w2_bf[:, 16 * D:32 * D], in_=w2_scr[:, 16 * D:32 * D]).then_inc(h("w2l"), 16)
                s.wait_ge(h("ssbv"), 1)

                def xn2f_ev(t):
                    s.wait_ge(h("sxn2t"), t + 1)
                    s.activation(
                        xn2_fm[:, :, (t % GT) * 128:(t % GT + 1) * 128],
                        ps_a[:].rearrange("p (k c) -> p k c", k=8),
                        AF.Identity).then_inc(h("sxn2f"), 1)

                def sqrt2_o1(t):
                    sl = t % 2
                    s.wait_ge(h("sbn2"), t + 1)
                    s.activation(sd_c[:, sl:sl + 1], mv_c[:, sl, 1:2], AF.Sqrt,
                                 bias=eps_col[:, 0:1]).then_inc(h("ssd2"), 1)
                    s.wait_ge(h("siv2"), t + 1)
                    if t >= 1:
                        s.wait_ge(h("ssy"), t)
                    s.activation(o1_sb[:], ps_a[:], AF.Identity,
                                 bias=nmi_c[:, sl:sl + 1], scale=inv_c[:, sl:sl + 1]).then_inc(h("so1"), 1)

                def yf_ev(t):
                    sl = t % 2
                    s.wait_ge(h("syt"), t + 1)
                    if t >= 2:
                        s.wait_ge(h("swom"), t - 1)
                    s.activation(yf_sb[:, sl, :], ps_b[:], AF.Identity).then_inc(h("syte"), 1)

                def ln2_tail(t):
                    sl = t % 2
                    ti = t % GT
                    s.wait_ge(h("sbn3"), t + 1)
                    s.activation(sd_c[:, sl:sl + 1], mv_c[:, sl, 1:2], AF.Sqrt,
                                 bias=eps_col[:, 0:1]).then_inc(h("ssd3"), 1)
                    s.wait_ge(h("siv3"), t + 1)
                    if t >= 2:
                        s.wait_ge(h("sxn2t"), t - 1)
                    if t == 2:
                        s.wait_ge(h("se1"), 64)
                        s.wait_ge(h("se2"), 256)
                    s.activation(xn2_tm[:, sl, :], x1_g[:, ti, :], AF.Identity,
                                 bias=nmi_c[:, sl:sl + 1], scale=inv_c[:, sl:sl + 1]).then_inc(h("sxn2"), 1)

                sqrt2_o1(0)
                for g in range(NG):
                    for ti in range(GT):
                        t = g * GT + ti
                        if ti < GT - 1:
                            yf_ev(t)
                            sqrt2_o1(t + 1)
                            if t >= 1 and (t - 1) % GT != GT - 1:
                                xn2f_ev(t - 1)
                            ln2_tail(t)
                        else:
                            yf_ev(t)
                            xn2f_ev(t - 1)
                            ln2_tail(t)
                            xn2f_ev(t)
                            if t + 1 < NT:
                                sqrt2_o1(t + 1)
                    s.wait_ge(h("sd4"), g + 1)
                    if g >= 1:
                        s.wait_ge(h("sd4r"), 16 * g)
                    s.activation(d4p[:], ps_sm[0:4, 0:512], AF.Identity,
                                 scale=scol[0:4, 0:1]).then_inc(h("sd4e"), 1)
                    for fc in range(32):
                        s.wait_ge(h("sw1cc"), 32 * g + fc + 1)
                        s.activation(h_fm[:, fc, :], ps_b[:, (fc % 2) * 512:(fc % 2 + 1) * 512],
                                     AF.Gelu, bias=b1c_sb[:, fc:fc + 1]).then_inc(h("sgel"), 1)
                    s.wait_ge(h("shd"), g + 1)
                    if g >= 1:
                        s.wait_ge(h("sw2t"), 4 * g)
                    s.activation(hd_sb[:], ps_sm[0:16, 512:1024], AF.Identity,
                                 scale=scol[0:16, 0:1]).then_inc(h("shde"), 1)

            @blk2.tensor
            def _(t_):
                t_.wait_ge(h("ssbv"), 1)
                t_.wait_ge(h("wol"), 16)
                t_.wait_ge(h("brl"), 48)

                def o_mm(t, xnf=None):
                    sl = t % 2
                    t_.wait_ge(h("ql%d" % sl), 16 * (t // 2 + 1))
                    if t % GT == 1 and t > 1:
                        t_.wait_ge(h("sw2e"), t - 1)
                    if t >= 1:
                        t_.wait_ge(h("so1"), t)
                    if t >= 2:
                        t_.wait_ge(h("sxn2f"), t - 2 if xnf is None else xnf)
                    for hh in range(HN):
                        mm = t_.matmul(ps_a[:, hh * 64:(hh + 1) * 64],
                                       qpt[:, sl, hh * 128:(hh + 1) * 128],
                                       s_sb[:, hh * 64:(hh + 1) * 64],
                                       start=True, stop=True)
                    mm.then_inc(h("somm"), 1)

                o_mm(0)
                for g in range(NG):
                    for ti in range(GT):
                        t = g * GT + ti
                        sl = t % 2
                        if ti < GT - 1 and t + 1 < NT:
                            o_mm(t + 1)
                        t_.wait_ge(h("ssy"), t + 1)
                        if t >= 1:
                            t_.wait_ge(h("syte"), t)
                        if ti == 0 and g >= 1:
                            t_.wait_ge(h("sgel"), 32 * g)
                        for c in range(8):
                            mm = t_.matmul(ps_b[:, c * 128:(c + 1) * 128],
                                           y_sb[:, c * 128:(c + 1) * 128], ident[:],
                                           start=True, stop=True)
                        mm.then_inc(h("syt"), 1)
                        t_.wait_ge(h("syte"), t + 1)
                        if t >= 1:
                            t_.wait_ge(h("sx1e"), t)
                        if ti == 0 and g >= 1:
                            t_.wait_ge(h("sw2e"), 4 * g)
                        for seg in range(2):
                            for kc in range(8):
                                t_.matmul(ps_c[:, seg * 512:(seg + 1) * 512],
                                          yf_sb[:, sl, kc * 128:(kc + 1) * 128],
                                          wo_bf[:, kc * D + seg * 512:kc * D + (seg + 1) * 512],
                                          start=(kc == 0), stop=False)
                            mm = t_.matmul(ps_c[:, seg * 512:(seg + 1) * 512],
                                           ones_row[0:1, 0:128],
                                           bo_row[0:1, seg * 512:(seg + 1) * 512],
                                           start=False, stop=True)
                        mm.then_inc(h("swom"), 1)
                        if t >= 1 and (t - 1) % GT != GT - 1:
                            tm = t - 1
                            t_.wait_ge(h("sxn2"), tm + 1)
                            if ti < GT - 1:
                                t_.wait_ge(h("so1"), t + 2)
                            else:
                                t_.wait_ge(h("sxn2f"), tm)
                            for c in range(8):
                                mm = t_.matmul(ps_a[:, c * 128:(c + 1) * 128],
                                               xn2_tm[:, tm % 2, c * 128:(c + 1) * 128], ident[:],
                                               start=True, stop=True)
                            mm.then_inc(h("sxn2t"), 1)
                    tb = g * GT + GT - 1
                    t_.wait_ge(h("sxn2"), tb + 1)
                    t_.wait_ge(h("sxn2f"), tb)
                    for c in range(8):
                        mm = t_.matmul(ps_a[:, c * 128:(c + 1) * 128],
                                       xn2_tm[:, tb % 2, c * 128:(c + 1) * 128], ident[:],
                                       start=True, stop=True)
                    mm.then_inc(h("sxn2t"), 1)
                    if g + 1 < NG:
                        o_mm((g + 1) * GT, xnf=(g + 1) * GT)
                    t_.wait_ge(h("sxn2f"), 4 * (g + 1))
                    if g == 0:
                        t_.wait_ge(h("sodd"), 1)
                    if g >= 1:
                        t_.wait_ge(h("sd4e"), g)
                    for kc in range(8):
                        mm = t_.matmul(ps_sm[0:4, 0:512], odd1_sb[:, kc, :], xn2_fm[:, kc, :],
                                       start=(kc == 0), stop=(kc == 7))
                    mm.then_inc(h("sd4"), 1)
                    if g == 0:
                        t_.wait_ge(h("se1"), 64)
                    t_.wait_ge(h("syte"), 4 * (g + 1))
                    t_.wait_ge(h("sxn2f"), 4 * (g + 1))
                    for fc in range(32):
                        idx = 32 * g + fc
                        i_fc = fc // 8
                        t_.wait_ge(h("wl%d" % (idx % 3)), 16 * (idx // 3 + 1))
                        if idx >= 2:
                            t_.wait_ge(h("sgel"), idx - 1)
                        psl = ps_b[:, (fc % 2) * 512:(fc % 2 + 1) * 512]
                        for kc in range(8):
                            t_.matmul(psl, w1sl[:, idx % 3, kc * 128:(kc + 1) * 128],
                                      xn2_fm[:, kc, :], start=(kc == 0), stop=False)
                        if fc == 0:
                            t_.wait_ge(h("sd4r"), 16 * (g + 1))
                        mm = t_.matmul(psl, even1_sb[0:1, i_fc, (fc % 8) * 128:(fc % 8 + 1) * 128],
                                       d4row[0:1, i_fc * 512:(i_fc + 1) * 512],
                                       start=False, stop=True)
                        mm.then_inc(h("sw1cc"), 1)
                        if fc == 31:
                            t_.nop().then_inc(h("sw1mg"), 1)
                        if fc == 2 and g == 0:
                            t_.wait_ge(h("sodd"), 2)
                        if fc == 2 and g >= 1:
                            t_.wait_ge(h("shde"), g)
                        if fc >= 2:
                            kch = fc - 2
                            t_.wait_ge(h("sgel"), 32 * g + kch + 1)
                            t_.matmul(ps_sm[0:16, 512:1024], odd2_sb[:, kch, :], h_fm[:, kch, :],
                                      start=(kch == 0), stop=False)
                    if g == 0:
                        t_.wait_ge(h("w2l"), 32)
                        t_.wait_ge(h("se2"), 256)
                    for kch in range(30, 32):
                        t_.wait_ge(h("sgel"), 32 * g + kch + 1)
                        mm = t_.matmul(ps_sm[0:16, 512:1024], odd2_sb[:, kch, :], h_fm[:, kch, :],
                                       start=False, stop=(kch == 31))
                    mm.then_inc(h("shd"), 1)
                    t_.wait_ge(h("shde"), g + 1)
                    for ti in range(GT):
                        t = g * GT + ti
                        psw = ps_c if ti % 2 == 0 else ps_a
                        if ti == 0:
                            t_.wait_ge(h("sx1e"), 4 * (g + 1))
                        elif ti == 1:
                            t_.wait_ge(h("so1"), min(t + 4, NT))
                        else:
                            t_.wait_ge(h("sw2e"), t - 1)
                        for seg in range(2):
                            for kc in range(32):
                                t_.matmul(psw[:, seg * 512:(seg + 1) * 512],
                                          h_fm[:, kc, ti * 128:(ti + 1) * 128],
                                          w2_bf[:, kc * D + seg * 512:kc * D + (seg + 1) * 512],
                                          start=(kc == 0), stop=False)
                            t_.matmul(psw[:, seg * 512:(seg + 1) * 512],
                                      ones_row[0:1, 0:128],
                                      b2_row[0:1, seg * 512:(seg + 1) * 512],
                                      start=False, stop=False)
                            mm = t_.matmul(psw[:, seg * 512:(seg + 1) * 512],
                                           hd_sb[:, ti * 128:(ti + 1) * 128],
                                           even2_sb[:, seg * 512:(seg + 1) * 512],
                                           start=False, stop=True)
                        mm.then_inc(h("sw2t"), 1)

            blk2.__exit__(None, None, None)

    return nc


def _host_stage(inputs):
    x = np.ascontiguousarray(inputs["L0_attn_x"], dtype=np.float32)
    Wq = np.asarray(inputs["L0_attn_Wqkvu"], dtype=np.float32)
    Wo = np.asarray(inputs["L0_attn_Wout"], dtype=np.float32)
    W1 = np.asarray(inputs["L0_ffn_W1"], dtype=np.float32)
    W2 = np.asarray(inputs["L0_ffn_W2"], dtype=np.float32)

    def pmaj(wt):
        K, F = wt.shape
        nk = K // 128
        return np.ascontiguousarray(wt.reshape(nk, 128, F).transpose(1, 0, 2).reshape(128, nk * F))

    wq = pmaj(np.ascontiguousarray(Wq.T))
    wo = pmaj(np.ascontiguousarray(Wo.T))
    w2 = pmaj(np.ascontiguousarray(W2.T))
    w1 = np.ascontiguousarray(
        W1.reshape(32, 128, 8, 128).transpose(3, 0, 2, 1).reshape(128, 32 * D))
    rep = lambda a: np.ascontiguousarray(
        np.broadcast_to(np.asarray(a, dtype=np.float32).reshape(1, -1), (128, a.size)))
    tc = np.ascontiguousarray(
        np.arange(NT, dtype=np.float32)[None, :] * 128.0
        + np.arange(128, dtype=np.float32)[:, None])
    shared = {
        "wq": wq, "wo": wo, "w1": w1, "w2": w2,
        "bq": np.asarray(inputs["L0_attn_bqkvu"], dtype=np.float32).reshape(1, H4),
        "bo": np.asarray(inputs["L0_attn_bout"], dtype=np.float32).reshape(1, D),
        "b1": np.asarray(inputs["L0_ffn_b1"], dtype=np.float32).reshape(1, H4),
        "b1c": np.ascontiguousarray(
            np.asarray(inputs["L0_ffn_b1"], dtype=np.float32).reshape(32, 128).T),
        "b2": np.asarray(inputs["L0_ffn_b2"], dtype=np.float32).reshape(1, D),
        "lnw": rep(np.asarray(inputs["L0_attn_ln_w"])),
        "lnb": rep(np.asarray(inputs["L0_attn_ln_b"])),
        "th": rep(np.asarray(inputs["L0_attn_theta"]).reshape(-1)),
        "tc": tc,
        "sc": np.ascontiguousarray(np.broadcast_to(
            np.asarray(inputs["L0_ffn_scale"], dtype=np.float32).reshape(1, 1), (128, 1))),
        "idm": np.eye(128, dtype=np.float32),
    }
    shared = {k: np.ascontiguousarray(v, dtype=np.float32) for k, v in shared.items()}
    return [dict(shared, x=np.ascontiguousarray(x[b])) for b in range(8)]


def kernel(**inputs):
    in_maps = _host_stage(inputs)
    nc = build_nc()
    res = run_bass_kernel_spmd(nc, in_maps, core_ids=list(range(8)))
    out = np.stack([np.asarray(r["y"]) for r in res.results], axis=0)
    return out.astype(np.float32)


# version 36
# speedup vs baseline: 1.0739x; 1.0023x over previous
import contextlib
import numpy as np

import concourse.bass as bass
import concourse.mybir as mybir
from concourse.bass_utils import run_bass_kernel_spmd

F32 = mybir.dt.float32
BF16 = mybir.dt.bfloat16
AF = mybir.ActivationFunctionType
ALU = mybir.AluOpType

D = 1024
HN = 16
NTOK = 2048
NT = 16
GT = 4
NG = NT // GT
H4 = 4096
EPS = 1e-5
PI = float(np.pi)
QO, KO, VO, UO = 0, D, 2 * D, 3 * D


def build_nc():
    nc = bass.Bass("TRN2", target_bir_lowering=False)

    x_h = nc.dram_tensor("x", [NTOK, D], F32, kind="ExternalInput")
    wq_h = nc.dram_tensor("wq", [128, 8 * H4], F32, kind="ExternalInput")
    wo_h = nc.dram_tensor("wo", [128, 8 * D], F32, kind="ExternalInput")
    w1_h = nc.dram_tensor("w1", [128, 32 * D], F32, kind="ExternalInput")
    w2_h = nc.dram_tensor("w2", [128, 32 * D], F32, kind="ExternalInput")
    bq_h = nc.dram_tensor("bq", [1, H4], F32, kind="ExternalInput")
    bo_h = nc.dram_tensor("bo", [1, D], F32, kind="ExternalInput")
    b1_h = nc.dram_tensor("b1", [1, H4], F32, kind="ExternalInput")
    b1c_h = nc.dram_tensor("b1c", [128, 32], F32, kind="ExternalInput")
    b2_h = nc.dram_tensor("b2", [1, D], F32, kind="ExternalInput")
    lnw_h = nc.dram_tensor("lnw", [128, D], F32, kind="ExternalInput")
    lnb_h = nc.dram_tensor("lnb", [128, D], F32, kind="ExternalInput")
    th_h = nc.dram_tensor("th", [128, D], F32, kind="ExternalInput")
    tc_h = nc.dram_tensor("tc", [128, NT], F32, kind="ExternalInput")
    sc_h = nc.dram_tensor("sc", [128, 1], F32, kind="ExternalInput")
    id_h = nc.dram_tensor("idm", [128, 128], F32, kind="ExternalInput")
    out_h = nc.dram_tensor("y", [NTOK, D], F32, kind="ExternalOutput")
    qp_scr = nc.dram_tensor("qp_scr", [NTOK, 2048], BF16, kind="Internal")
    u_scr = nc.dram_tensor("u_scr", [NTOK, D], BF16, kind="Internal")
    w1_scr = nc.dram_tensor("w1_scr", [128, 32 * D], BF16, kind="Internal")
    w2_scr = nc.dram_tensor("w2_scr", [128, 32 * D], BF16, kind="Internal")

    es = contextlib.ExitStack()
    with es:
        E = lambda cm: es.enter_context(cm)
        ident = E(nc.sbuf_tensor("ident", [128, 128], BF16))
        scol = E(nc.sbuf_tensor("scol", [128, 1], F32))
        lnw_rep = E(nc.sbuf_tensor("lnw_rep", [128, D], BF16))
        lnb_rep = E(nc.sbuf_tensor("lnb_rep", [128, D], BF16))
        ones_row = E(nc.sbuf_tensor("ones_row", [1, 512], BF16))
        pi_col = E(nc.sbuf_tensor("pi_col", [128, 1], F32))
        p32_col = E(nc.sbuf_tensor("p32_col", [128, 1], F32))
        npi_col = E(nc.sbuf_tensor("npi_col", [128, 1], F32))
        eps_col = E(nc.sbuf_tensor("eps_col", [128, 1], F32))
        wo_bf = E(nc.sbuf_tensor("wo_bf", [128, 8 * D], BF16))
        w1sl = E(nc.sbuf_tensor("w1sl", [128, 3, D], BF16))
        s_sb = E(nc.sbuf_tensor("s_sb", [128, D], BF16))
        mv_c = E(nc.sbuf_tensor("mv_c", [128, 2, 2], F32))
        sd_c = E(nc.sbuf_tensor("sd_c", [128, 2], F32))
        inv_c = E(nc.sbuf_tensor("inv_c", [128, 2], F32))
        nmi_c = E(nc.sbuf_tensor("nmi_c", [128, 2], F32))
        bn6 = E(nc.sbuf_tensor("bn6", [128, 2, 12], F32))
        ps_a = E(nc.psum_tensor("ps_a", [128, D], F32))
        ps_b = E(nc.psum_tensor("ps_b", [128, D], F32))
        ps_c = E(nc.psum_tensor("ps_c", [128, D], F32))

        sems = {}
        for n in ["pro", "proc", "wql", "wol",
                  "xl", "sbn", "ssd", "siv", "sxn", "sxnt", "sxnf",
                  "sk4", "sv4", "su4", "sq4", "sks", "sqs",
                  "stbl", "smod", "ssc", "skp", "sqp",
                  "sve", "sue", "ssm", "st1", "st2", "se1v", "sqpf",
                  "sust", "sqst", "w1c", "w1s",
                  "w2l", "ssbv", "ql", "ul", "xl2",
                  "somm", "sbn2", "ssd2", "siv2", "so1", "ssy",
                  "wq0", "wq1", "wq2", "wq3",
                  "syt", "syte", "swom", "sx1e",
                  "sbn3", "ssd3", "siv3", "sxn2", "sxn2t", "sxn2f",
                  "se1", "se2", "sodd", "sd4", "sd4e", "sd4r",
                  "sw1mg", "sw1ld", "sw1cc", "sgel", "shd", "shde",
                  "sw2t", "sw2e", "sout", "brl",
                  "xl0", "xl1", "su0", "su1", "sq0", "sq1", "ws0", "ws1",
                  "ql0", "ql1", "xa0", "xa1", "wl0", "wl1", "wl2",
                  "sot0", "sot1", "sot2", "sot3"]:
            sems[n] = E(nc.semaphore("s_" + n))
        h = lambda n: sems[n]

        PRO_TOT = 16 * 7

        s1 = contextlib.ExitStack()
        with s1:
            E1 = lambda cm: s1.enter_context(cm)
            wq_bf = E1(nc.sbuf_tensor("wq_bf", [128, 8 * H4], BF16))
            bq_row = E1(nc.sbuf_tensor("bq_row", [1, H4], BF16))
            th_rep = E1(nc.sbuf_tensor("th_rep", [128, D], F32))
            tcol = E1(nc.sbuf_tensor("tcol", [128, NT], F32))
            xbuf = E1(nc.sbuf_tensor("xbuf", [128, 2, D], F32))
            xn_tm = E1(nc.sbuf_tensor("xn_tm", [128, 2, D], BF16))
            xn_fm = E1(nc.sbuf_tensor("xn_fm", [128, 2, D], BF16))
            ks_sb = E1(nc.sbuf_tensor("ks_sb", [128, 2, D], BF16))
            qs_sb = E1(nc.sbuf_tensor("qs_sb", [128, 2, D], BF16))
            v_sb = E1(nc.sbuf_tensor("v_sb", [128, 2, D], BF16))
            u_sb = E1(nc.sbuf_tensor("u_sb", [128, 2, D], BF16))
            ang_sb = E1(nc.sbuf_tensor("ang_sb", [128, 2, 2, D], F32))
            dlt_sb = E1(nc.sbuf_tensor("dlt_sb", [128, D], F32))
            ki_sb = E1(nc.sbuf_tensor("ki_sb", [128, D], mybir.dt.int32))
            cos_sb = E1(nc.sbuf_tensor("cos_sb", [128, 2, D], BF16))
            sin_sb = E1(nc.sbuf_tensor("sin_sb", [128, 2, D], BF16))
            kp_sb = E1(nc.sbuf_tensor("kp_sb", [128, 2, 2048], BF16))
            qp_sb = E1(nc.sbuf_tensor("qp_sb", [128, 2, 2048], BF16))
            qf_sb = E1(nc.sbuf_tensor("qf_sb", [128, 2, 2048], BF16))
            w1st = E1(nc.sbuf_tensor("w1st", [128, 2, D], BF16))
            s_ps = E1(nc.psum_tensor("s_ps", [128, D], F32))

            hview = lambda ap: ap.rearrange("p (hh e) -> p hh e", hh=HN)

            blk1 = nc.Block("b1")
            blk1.__enter__()

            @blk1.gpsimd
            def _(g):
                g.memset(ones_row[:], 1.0)
                g.memset(pi_col[:], 2 * PI)
                g.memset(p32_col[:], 0.5 * PI)
                g.memset(npi_col[:], -2 * PI)
                g.memset(eps_col[:], EPS).then_inc(h("proc"), 1)
                g.dma_start(out=th_rep[:], in_=th_h[:]).then_inc(h("pro"), 16)
                g.dma_start(out=tcol[:], in_=tc_h[:]).then_inc(h("pro"), 16)
                g.dma_start(out=scol[:], in_=sc_h[:]).then_inc(h("pro"), 16)
                g.dma_start(out=ident[:], in_=id_h[:]).then_inc(h("pro"), 16)
                g.dma_start(out=lnw_rep[:], in_=lnw_h[:]).then_inc(h("pro"), 16)
                g.dma_start(out=lnb_rep[:], in_=lnb_h[:]).then_inc(h("pro"), 16)
                g.dma_start(out=bq_row[:], in_=bq_h[:]).then_inc(h("pro"), 16)
                for wchunk in range(4):
                    g.dma_start(out=wq_bf[:, wchunk * 2 * H4:(wchunk + 1) * 2 * H4],
                                in_=wq_h[:, wchunk * 2 * H4:(wchunk + 1) * 2 * H4]).then_inc(h("wq%d" % wchunk), 16)
                g.dma_start(out=wo_bf[:], in_=wo_h[:]).then_inc(h("wol"), 16)
                for fc in range(64):
                    sl = fc % 2
                    src = w1_h if fc < 32 else w2_h
                    dst = w1_scr if fc < 32 else w2_scr
                    c = fc % 32
                    if fc >= 2:
                        g.wait_ge(h("ws%d" % sl), 16 * (fc // 2))
                    g.dma_start(out=w1st[:, sl, :], in_=src[:, c * D:(c + 1) * D]).then_inc(h("w1c"), 16)
                    g.wait_ge(h("w1c"), 16 * (fc + 1))
                    g.dma_start(out=dst[:, c * D:(c + 1) * D], in_=w1st[:, sl, :]).then_inc(h("ws%d" % sl), 16)
                g.wait_ge(h("ws0"), 16 * 32)
                g.wait_ge(h("ws1"), 16 * 32)

            @blk1.sync
            def _(sy):
                for t in range(NT + 1):
                    if t < NT:
                        if t >= 2:
                            sy.wait_ge(h("sxn"), t - 1)
                        sy.dma_start(out=xbuf[:, t % 2, :],
                                     in_=x_h[t * 128:(t + 1) * 128, :]).then_inc(h("xl%d" % (t % 2)), 16)
                    if t >= 1:
                        tp = t - 1
                        sy.wait_ge(h("sue"), tp + 1)
                        sy.dma_start(out=u_scr[tp * 128:(tp + 1) * 128, :],
                                     in_=u_sb[:, tp % 2, :]).then_inc(h("su%d" % (tp % 2)), 16)
                        sy.wait_ge(h("sqpf"), tp + 1)
                        sy.dma_start(out=qp_scr[tp * 128:(tp + 1) * 128, :],
                                     in_=qf_sb[:, tp % 2, :]).then_inc(h("sq%d" % (tp % 2)), 16)
                sy.wait_ge(h("ws0"), 16)
                sy.dma_start(out=w1sl[:, 0, :], in_=w1_scr[:, 0:D]).then_inc(h("wl0"), 16)
                sy.wait_ge(h("ws1"), 16)
                sy.dma_start(out=w1sl[:, 1, :], in_=w1_scr[:, D:2 * D]).then_inc(h("wl1"), 16)
                sy.wait_ge(h("su0"), 16 * 8)
                sy.wait_ge(h("su1"), 16 * 8)
                sy.wait_ge(h("sq0"), 16 * 8)
                sy.wait_ge(h("sq1"), 16 * 8)

            @blk1.vector
            def _(v):
                v.wait_ge(h("pro"), PRO_TOT)
                v.wait_ge(h("proc"), 1)
                v.tensor_scalar_mul(dlt_sb[:], th_rep[:], 6.4206382432985265e-09)
                v.drain()
                v.scalar_tensor_tensor(th_rep[:], th_rep[:], 0.15915493667125702,
                                       dlt_sb[:], ALU.mult, ALU.add).then_inc(h("smod"), 1)

                def prep_a(t):
                    sl = t % 2
                    v.wait_ge(h("xl%d" % sl), 16 * (t // 2 + 1))
                    v.bn_stats(bn6[:, sl, 0:6], xbuf[:, sl, 0:512])
                    v.bn_stats(bn6[:, sl, 6:12], xbuf[:, sl, 512:1024])
                    v.drain()
                    v.bn_aggr(mv_c[:, sl, :], bn6[:, sl, :]).then_inc(h("sbn"), 1)

                def prep_b(t):
                    sl = t % 2
                    v.wait_ge(h("ssd"), t + 1)
                    v.reciprocal(inv_c[:, sl:sl + 1], sd_c[:, sl:sl + 1])
                    v.drain()
                    v.tensor_scalar(nmi_c[:, sl:sl + 1], mv_c[:, sl, 0:1],
                                    inv_c[:, sl:sl + 1], -1.0, ALU.mult, ALU.mult).then_inc(h("siv"), 1)

                def prep(t):
                    prep_a(t)
                    prep_b(t)

                def wrap_cos(sl):
                    v.scalar_tensor_tensor(ang_sb[:, sl, 1, :], ang_sb[:, sl, 0, :], 0.5,
                                           ang_sb[:, sl, 0, :], ALU.is_ge, ALU.subtract)
                    v.drain()
                    v.scalar_tensor_tensor(ang_sb[:, sl, 0, :], ang_sb[:, sl, 1, :], 0.5,
                                           ang_sb[:, sl, 1, :], ALU.is_ge, ALU.subtract)
                    v.drain()
                    v.scalar_tensor_tensor(ang_sb[:, sl, 1, :], ang_sb[:, sl, 0, :], 0.25,
                                           ang_sb[:, sl, 0, :], ALU.is_ge, ALU.subtract)
                    v.drain(fusable=True).then_inc(h("smod"), 1)

                def chain0():
                    v.wait_ge(h("stbl"), 1)
                    v.tensor_copy(ki_sb[:], ang_sb[:, 0, 0, :])
                    v.drain()
                    v.tensor_copy(ang_sb[:, 0, 1, :], ki_sb[:])
                    v.drain()
                    v.scalar_tensor_tensor(ang_sb[:, 0, 0, :], ang_sb[:, 0, 1, :], -1.0,
                                           ang_sb[:, 0, 0, :], ALU.mult, ALU.add)
                    v.drain()
                    wrap_cos(0)
                    v.tensor_scalar_mul(ang_sb[:, 1, 0, :], th_rep[:], 128.0)
                    v.drain()
                    v.tensor_copy(ki_sb[:], ang_sb[:, 1, 0, :])
                    v.drain()
                    v.tensor_copy(dlt_sb[:], ki_sb[:])
                    v.drain()
                    v.scalar_tensor_tensor(dlt_sb[:], dlt_sb[:], -1.0,
                                           ang_sb[:, 1, 0, :], ALU.mult, ALU.add)
                    v.drain()
                    v.scalar_tensor_tensor(ang_sb[:, 1, 0, :], dlt_sb[:], 0.5,
                                           dlt_sb[:], ALU.is_ge, ALU.subtract)
                    v.drain()
                    v.scalar_tensor_tensor(dlt_sb[:], ang_sb[:, 1, 0, :], 0.5,
                                           ang_sb[:, 1, 0, :], ALU.is_ge, ALU.subtract)
                    v.drain()

                def chain(t):
                    sl = t % 2
                    if t >= 2:
                        v.wait_ge(h("ssc"), t - 1)
                    v.tensor_add(ang_sb[:, sl, 0, :], ang_sb[:, 1 - sl, 0, :], dlt_sb[:])
                    v.drain()
                    wrap_cos(sl)

                prep(0)
                chain0()
                for t in range(NT):
                    sl = t % 2
                    if t + 1 < NT:
                        prep_a(t + 1)
                    v.wait_ge(h("sks"), t + 1)
                    v.wait_ge(h("ssc"), t + 1)
                    if t >= 2:
                        v.wait_ge(h("ssm"), t - 1)
                    v.tensor_mul(hview(kp_sb[:, sl, :])[:, :, 0:64],
                                 hview(ks_sb[:, sl, :]), hview(cos_sb[:, sl, :]))
                    v.tensor_mul(hview(kp_sb[:, sl, :])[:, :, 64:128],
                                 hview(ks_sb[:, sl, :]), hview(sin_sb[:, sl, :])).then_inc(h("skp"), 1)
                    v.wait_ge(h("sv4"), t + 1)
                    v.tensor_copy(v_sb[:, sl, :], ps_c[:]).then_inc(h("sve"), 1)
                    v.wait_ge(h("su4"), t + 1)
                    if t >= 2:
                        v.wait_ge(h("su%d" % sl), 16 * (t // 2))
                    v.tensor_copy(u_sb[:, sl, :], ps_b[:]).then_inc(h("sue"), 1)
                    v.wait_ge(h("sqs"), t + 1)
                    if t >= 2:
                        v.wait_ge(h("st2"), t - 1)
                    v.tensor_mul(hview(qp_sb[:, sl, :])[:, :, 0:64],
                                 hview(qs_sb[:, sl, :]), hview(cos_sb[:, sl, :]))
                    v.tensor_mul(hview(qp_sb[:, sl, :])[:, :, 64:128],
                                 hview(qs_sb[:, sl, :]), hview(sin_sb[:, sl, :])).then_inc(h("sqp"), 1)
                    if t + 1 < NT:
                        prep_b(t + 1)
                    v.wait_ge(h("st1"), t + 1)
                    if t >= 2:
                        v.wait_ge(h("sq%d" % sl), 16 * (t // 2))
                    v.tensor_copy(qf_sb[:, sl, 0:1024], ps_b[:]).then_inc(h("se1v"), 1)
                    v.wait_ge(h("st2"), t + 1)
                    v.tensor_copy(qf_sb[:, sl, 1024:2048], ps_c[:]).then_inc(h("sqpf"), 1)
                    if t + 1 < NT:
                        chain(t + 1)

            @blk1.scalar
            def _(s):
                s.wait_ge(h("pro"), PRO_TOT)
                s.wait_ge(h("proc"), 1)
                s.wait_ge(h("smod"), 1)
                s.activation(ang_sb[:, 0, 0, :], th_rep[:], AF.Identity,
                             scale=tcol[:, 0:1]).then_inc(h("stbl"), 1)
                s.wait_ge(h("sbn"), 1)
                s.activation(sd_c[:, 0:1], mv_c[:, 0, 1:2], AF.Sqrt,
                             bias=eps_col[:, 0:1]).then_inc(h("ssd"), 1)
                for t in range(NT):
                    sl = t % 2
                    s.wait_ge(h("siv"), t + 1)
                    s.wait_ge(h("xl%d" % sl), 16 * (t // 2 + 1))
                    if t >= 2:
                        s.wait_ge(h("sxnt"), t - 1)
                    s.activation(xn_tm[:, sl, :], xbuf[:, sl, :], AF.Identity,
                                 bias=nmi_c[:, sl:sl + 1], scale=inv_c[:, sl:sl + 1]).then_inc(h("sxn"), 1)
                    s.wait_ge(h("sxnt"), t + 1)
                    if t >= 2:
                        s.wait_ge(h("sq4"), t - 1)
                    s.activation(xn_fm[:, sl, 0:512], ps_a[:, 0:512],
                                 AF.Identity).then_inc(h("sxnf"), 1)
                    s.activation(xn_fm[:, sl, 512:1024], ps_a[:, 512:1024],
                                 AF.Identity).then_inc(h("sxnf"), 1)
                    s.wait_ge(h("smod"), t + 2)
                    s.activation(sin_sb[:, sl, :], ang_sb[:, sl, 0, :], AF.Sin, scale=pi_col[:, 0:1])
                    s.activation(cos_sb[:, sl, :], ang_sb[:, sl, 1, :], AF.Sin,
                                 bias=p32_col[:, 0:1], scale=npi_col[:, 0:1]).then_inc(h("ssc"), 1)
                    s.wait_ge(h("sk4"), t + 1)
                    if t >= 2:
                        s.wait_ge(h("skp"), t - 1)
                    s.activation(ks_sb[:, sl, :], ps_b[:], AF.Silu).then_inc(h("sks"), 1)
                    s.wait_ge(h("sq4"), t + 1)
                    if t >= 2:
                        s.wait_ge(h("sqp"), t - 1)
                    s.activation(qs_sb[:, sl, :], ps_c[:], AF.Silu).then_inc(h("sqs"), 1)
                    if t + 1 < NT:
                        slp = (t + 1) % 2
                        s.wait_ge(h("sbn"), t + 2)
                        s.activation(sd_c[:, slp:slp + 1], mv_c[:, slp, 1:2], AF.Sqrt,
                                     bias=eps_col[:, 0:1]).then_inc(h("ssd"), 1)

            @blk1.tensor
            def _(t_):
                t_.wait_ge(h("pro"), PRO_TOT)
                t_.wait_ge(h("proc"), 1)

                def quarter(qoff, sl, psd, sem, first, xnf_t=None):
                    for seg in range(2):
                        for kc in range(8):
                            if first and seg == 0:
                                t_.wait_ge(h("wq%d" % (kc // 2)), 16)
                            if xnf_t is not None and seg == 0 and kc in (0, 4):
                                t_.wait_ge(h("sxnf"), 2 * xnf_t + 1 + (kc // 4))
                            t_.matmul(psd[:, seg * 512:(seg + 1) * 512],
                                      xn_fm[:, sl, kc * 128:(kc + 1) * 128],
                                      wq_bf[:, kc * H4 + qoff + seg * 512:
                                            kc * H4 + qoff + (seg + 1) * 512],
                                      start=(kc == 0), stop=False)
                        mm = t_.matmul(psd[:, seg * 512:(seg + 1) * 512],
                                       ones_row[0:1, 0:128],
                                       bq_row[0:1, qoff + seg * 512:qoff + (seg + 1) * 512],
                                       start=False, stop=True)
                    mm.then_inc(sem, 1)

                for t in range(NT):
                    sl = t % 2
                    t_.wait_ge(h("sxn"), t + 1)
                    if t >= 1:
                        t_.wait_ge(h("sxnf"), 2 * t)
                    for c in range(8):
                        mm = t_.matmul(ps_a[:, c * 128:(c + 1) * 128],
                                       xn_tm[:, sl, c * 128:(c + 1) * 128], ident[:],
                                       start=True, stop=True)
                    mm.then_inc(h("sxnt"), 1)
                    if t >= 1:
                        t_.wait_ge(h("se1v"), t)
                    quarter(KO, sl, ps_b, h("sk4"), t == 0, xnf_t=t)
                    if t >= 1:
                        t_.wait_ge(h("sqpf"), t)
                    quarter(VO, sl, ps_c, h("sv4"), False)
                    t_.wait_ge(h("sks"), t + 1)
                    quarter(UO, sl, ps_b, h("su4"), False)
                    t_.wait_ge(h("sve"), t + 1)
                    quarter(QO, sl, ps_c, h("sq4"), False)
                    t_.wait_ge(h("skp"), t + 1)
                    for hh in range(HN):
                        mm = t_.matmul(s_ps[:, hh * 64:(hh + 1) * 64],
                                       kp_sb[:, sl, hh * 128:(hh + 1) * 128],
                                       v_sb[:, sl, hh * 64:(hh + 1) * 64],
                                       start=(t == 0 and hh % 8 == 0),
                                       stop=(t == NT - 1 and hh % 8 == 7))
                    mm.then_inc(h("ssm"), 1)
                    t_.wait_ge(h("sqp"), t + 1)
                    t_.wait_ge(h("sue"), t + 1)
                    for c in range(8):
                        mm = t_.matmul(ps_b[:, c * 128:(c + 1) * 128],
                                       qp_sb[:, sl, c * 128:(c + 1) * 128], ident[:],
                                       start=True, stop=True)
                    mm.then_inc(h("st1"), 1)
                    t_.wait_ge(h("sqs"), t + 1)
                    for c in range(8):
                        mm = t_.matmul(ps_c[:, c * 128:(c + 1) * 128],
                                       qp_sb[:, sl, 1024 + c * 128:1024 + (c + 1) * 128], ident[:],
                                       start=True, stop=True)
                    mm.then_inc(h("st2"), 1)

            blk1.__exit__(None, None, None)
            blk1b = nc.Block("b1b")
            blk1b.__enter__()

            @blk1b.vector
            def _(v):
                v.tensor_copy(s_sb[:], s_ps[:]).then_inc(h("ssbv"), 1)

            blk1b.__exit__(None, None, None)

        s2 = contextlib.ExitStack()
        with s2:
            E2 = lambda cm: s2.enter_context(cm)
            w2_bf = E2(nc.sbuf_tensor("w2_bf", [128, 32 * D], BF16))
            b1c_sb = E2(nc.sbuf_tensor("b1c_sb", [128, 32], F32))
            bo_row = E2(nc.sbuf_tensor("bo_row", [1, D], BF16))
            b2_row = E2(nc.sbuf_tensor("b2_row", [1, D], BF16))
            odd1_sb = E2(nc.sbuf_tensor("odd1_sb", [128, 8, 4], BF16))
            even1_sb = E2(nc.sbuf_tensor("even1_sb", [1, 4, D], BF16))
            even2_sb = E2(nc.sbuf_tensor("even2_sb", [16, D], BF16))
            odd2_sb = E2(nc.sbuf_tensor("odd2_sb", [128, 32, 16], BF16))
            qpt = E2(nc.sbuf_tensor("qpt", [128, 2, 2048], BF16))
            ut = E2(nc.sbuf_tensor("ut", [128, 1, D], BF16))
            xt = E2(nc.sbuf_tensor("xt", [128, 2, D], F32))
            o1_sb = E2(nc.sbuf_tensor("o1_sb", [128, D], F32))
            y_sb = E2(nc.sbuf_tensor("y_sb", [128, D], BF16))
            yf_sb = E2(nc.sbuf_tensor("yf_sb", [128, 2, D], BF16))
            x1_g = E2(nc.sbuf_tensor("x1_g", [128, GT, D], F32))
            xn2_tm = E2(nc.sbuf_tensor("xn2_tm", [128, 2, D], BF16))
            xn2_fm = E2(nc.sbuf_tensor("xn2_fm", [128, 8, 512], BF16))
            h_fm = E2(nc.sbuf_tensor("h_fm", [128, 32, 512], BF16))
            d4p = E2(nc.sbuf_tensor("d4p", [4, 512], BF16))
            d4row = E2(nc.sbuf_tensor("d4row", [1, 2048], BF16))
            hd_sb = E2(nc.sbuf_tensor("hd_sb", [16, 512], BF16))
            ps_sm = E2(nc.psum_tensor("ps_sm", [16, 1024], F32))

            blk2 = nc.Block("b2")
            blk2.__enter__()

            @blk2.gpsimd
            def _(g_):
                g_.dma_start(out=b1c_sb[:], in_=b1c_h[:]).then_inc(h("brl"), 16)
                g_.dma_start(out=bo_row[:], in_=bo_h[:]).then_inc(h("brl"), 16)
                g_.dma_start(out=b2_row[:], in_=b2_h[:]).then_inc(h("brl"), 16)
                g_.memset(even2_sb[:], 0.0)
                g_.drain()
                g_.wait_ge(h("sxn2"), 1)
                for i in range(4):
                    g_.dma_start(out=even1_sb[0:1, i, :],
                                 in_=xn2_tm[1 + 2 * i:2 + 2 * i, 0, :]).then_inc(h("se1"), 16)
                for i in range(4):
                    for l in range(4):
                        g_.dma_start(out=even2_sb[4 * i + l:4 * i + l + 1, 256 * i:256 * (i + 1)],
                                     in_=xn2_tm[9 + 2 * i:10 + 2 * i, 0, l:D:4],
                                     allow_slow_non_contiguous=True).then_inc(h("se2"), 16)
                g_.wait_ge(h("ws0"), 32)
                g_.dma_start(out=w1sl[:, 2, :],
                             in_=w1_scr[:, 2 * D:3 * D]).then_inc(h("wl2"), 16)
                g_.wait_ge(h("sd4e"), 1)
                g_.dma_start(out=d4row[0:1, :], in_=d4p[:]).then_inc(h("sd4r"), 16)
                for g in range(NG):
                    if g >= 1:
                        g_.wait_ge(h("sd4e"), g + 1)
                        g_.wait_ge(h("sw1mg"), g)
                        g_.dma_start(out=d4row[0:1, :], in_=d4p[:]).then_inc(h("sd4r"), 16)
                    for fc in range(32):
                        idx = 32 * g + fc
                        if idx < 3 or idx % 3 != 2:
                            continue
                        g_.wait_ge(h("sw1cc"), idx - 2)
                        g_.dma_start(out=w1sl[:, 2, :],
                                     in_=w1_scr[:, fc * D:(fc + 1) * D]).then_inc(h("wl2"), 16)
                g_.wait_ge(h("sd4r"), 16 * NG)
                g_.wait_ge(h("se1"), 64)
                g_.wait_ge(h("se2"), 256)

            @blk2.sync
            def _(sy):
                def qp_load(t):
                    sl = t % 2
                    if t >= 2:
                        sy.wait_ge(h("somm"), t - 1)
                    sy.dma_start(out=qpt[:, sl, :],
                                 in_=qp_scr[t * 128:(t + 1) * 128, :]).then_inc(h("ql%d" % sl), 16)

                def u_load(t):
                    if t >= 1:
                        sy.wait_ge(h("ssy"), t)
                    sy.dma_start(out=ut[:, 0, :],
                                 in_=u_scr[t * 128:(t + 1) * 128, :]).then_inc(h("ul"), 16)

                def x_load(t):
                    sl = t % 2
                    if t >= 2:
                        sy.wait_ge(h("sx1e"), t - 1)
                    sy.dma_start(out=xt[:, sl, :],
                                 in_=x_h[t * 128:(t + 1) * 128, :]).then_inc(h("xa%d" % sl), 16)

                for g in range(NG):
                    if g >= 1:
                        for ti in range(GT):
                            tp = (g - 1) * GT + ti
                            sy.wait_ge(h("sw2e"), tp + 1)
                            sy.dma_start(out=out_h[tp * 128:(tp + 1) * 128, :],
                                         in_=x1_g[:, tp % GT, :]).then_inc(h("sot%d" % (tp % GT)), 16)
                    for ti in range(GT):
                        t = g * GT + ti
                        if g == 0 or ti >= 2:
                            qp_load(t)
                        if g == 0 or ti >= 1:
                            u_load(t)
                            x_load(t)
                    if g + 1 < NG:
                        qp_load(g * GT + GT)
                        qp_load(g * GT + GT + 1)
                        u_load(g * GT + GT)
                        x_load(g * GT + GT)
                    for fc in range(32):
                        idx = 32 * g + fc
                        if idx < 3 or idx % 3 == 2:
                            continue
                        sy.wait_ge(h("sw1cc"), idx - 2)
                        sy.dma_start(out=w1sl[:, idx % 3, :],
                                     in_=w1_scr[:, fc * D:(fc + 1) * D]).then_inc(h("wl%d" % (idx % 3)), 16)
                for ti in range(GT):
                    tp = (NG - 1) * GT + ti
                    sy.wait_ge(h("sw2e"), tp + 1)
                    sy.dma_start(out=out_h[tp * 128:(tp + 1) * 128, :],
                                 in_=x1_g[:, tp % GT, :]).then_inc(h("sot%d" % (tp % GT)), 16)
                for q4 in range(GT):
                    sy.wait_ge(h("sot%d" % q4), 16 * NG)

            @blk2.vector
            def _(v):
                def stats2a(t):
                    sl = t % 2
                    v.wait_ge(h("somm"), t + 1)
                    v.bn_stats(bn6[:, sl, 0:6], ps_a[:, 0:512])
                    v.bn_stats(bn6[:, sl, 6:12], ps_a[:, 512:1024])
                    v.drain()
                    v.bn_aggr(mv_c[:, sl, :], bn6[:, sl, :]).then_inc(h("sbn2"), 1)

                def stats2b(t):
                    sl = t % 2
                    v.wait_ge(h("ssd2"), t + 1)
                    v.reciprocal(inv_c[:, sl:sl + 1], sd_c[:, sl:sl + 1])
                    v.drain()
                    v.tensor_scalar(nmi_c[:, sl:sl + 1], mv_c[:, sl, 0:1],
                                    inv_c[:, sl:sl + 1], -1.0, ALU.mult, ALU.mult).then_inc(h("siv2"), 1)

                def stats2(t):
                    stats2a(t)
                    stats2b(t)

                def ychain(t):
                    v.wait_ge(h("so1"), t + 1)
                    v.wait_ge(h("ul"), 16 * (t + 1))
                    if t >= 1:
                        v.wait_ge(h("syt"), t)
                    v.tensor_mul(o1_sb[:], o1_sb[:], lnw_rep[:])
                    v.drain()
                    v.tensor_add(o1_sb[:], o1_sb[:], lnb_rep[:])
                    v.drain()
                    v.tensor_mul(y_sb[:], o1_sb[:], ut[:, 0, :]).then_inc(h("ssy"), 1)

                def x1ev(t):
                    sl = t % 2
                    ti = t % GT
                    v.wait_ge(h("swom"), t + 1)
                    v.wait_ge(h("xa%d" % sl), 16 * (t // 2 + 1))
                    if t >= GT:
                        v.wait_ge(h("sot%d" % ti), 16 * (t // GT))
                    v.tensor_add(x1_g[:, ti, :], ps_c[:], xt[:, sl, :])
                    v.drain(fusable=True).then_inc(h("sx1e"), 1)

                def ln2(t):
                    sl = t % 2
                    ti = t % GT
                    v.bn_stats(bn6[:, sl, 0:6], x1_g[:, ti, 0:512])
                    v.bn_stats(bn6[:, sl, 6:12], x1_g[:, ti, 512:1024])
                    v.drain()
                    v.bn_aggr(mv_c[:, sl, :], bn6[:, sl, :]).then_inc(h("sbn3"), 1)
                    v.wait_ge(h("ssd3"), t + 1)
                    v.reciprocal(inv_c[:, sl:sl + 1], sd_c[:, sl:sl + 1])
                    v.drain()
                    v.tensor_scalar(nmi_c[:, sl:sl + 1], mv_c[:, sl, 0:1],
                                    inv_c[:, sl:sl + 1], -1.0, ALU.mult, ALU.mult).then_inc(h("siv3"), 1)

                stats2(0)
                ychain(0)
                for g in range(NG):
                    for ti in range(GT):
                        t = g * GT + ti
                        if ti < GT - 1:
                            stats2a(t + 1)
                            stats2b(t + 1)
                            ychain(t + 1)
                            x1ev(t)
                            ln2(t)
                        else:
                            x1ev(t)
                            ln2(t)
                            if t + 1 < NT:
                                stats2a(t + 1)
                                stats2b(t + 1)
                        if t == 1:
                            v.wait_ge(h("sxn2f"), 1)
                            v.drain()
                            for kc in range(8):
                                cp = v.tensor_copy(odd1_sb[:, kc, :], xn2_fm[:, kc, 2:9:2])
                            cp.then_inc(h("sodd"), 1)
                            v.memset(odd2_sb[:], 0.0)
                            v.drain()
                            for kc in range(32):
                                l = kc // 8
                                cp = v.tensor_copy(odd2_sb[:, kc, l:16:4], xn2_fm[:, kc % 8, 10:17:2])
                            cp.then_inc(h("sodd"), 1)
                    if (g + 1) * GT < NT:
                        ychain((g + 1) * GT)
                    for ti in range(GT):
                        t = g * GT + ti
                        psw = ps_c if ti % 2 == 0 else ps_a
                        v.wait_ge(h("sw2t"), t + 1)
                        v.tensor_add(x1_g[:, ti, :], psw[:], x1_g[:, ti, :]).then_inc(h("sw2e"), 1)


            @blk2.scalar
            def _(s):
                s.dma_start(out=w2_bf[:, 0:16 * D], in_=w2_scr[:, 0:16 * D]).then_inc(h("w2l"), 16)
                s.dma_start(out=w2_bf[:, 16 * D:32 * D], in_=w2_scr[:, 16 * D:32 * D]).then_inc(h("w2l"), 16)
                s.wait_ge(h("ssbv"), 1)

                def xn2f_ev(t):
                    s.wait_ge(h("sxn2t"), t + 1)
                    s.activation(
                        xn2_fm[:, :, (t % GT) * 128:(t % GT + 1) * 128],
                        ps_a[:].rearrange("p (k c) -> p k c", k=8),
                        AF.Identity).then_inc(h("sxn2f"), 1)

                def sqrt2_o1(t):
                    sl = t % 2
                    s.wait_ge(h("sbn2"), t + 1)
                    s.activation(sd_c[:, sl:sl + 1], mv_c[:, sl, 1:2], AF.Sqrt,
                                 bias=eps_col[:, 0:1]).then_inc(h("ssd2"), 1)
                    s.wait_ge(h("siv2"), t + 1)
                    if t >= 1:
                        s.wait_ge(h("ssy"), t)
                    s.activation(o1_sb[:], ps_a[:], AF.Identity,
                                 bias=nmi_c[:, sl:sl + 1], scale=inv_c[:, sl:sl + 1]).then_inc(h("so1"), 1)

                def yf_ev(t):
                    sl = t % 2
                    s.wait_ge(h("syt"), t + 1)
                    if t >= 2:
                        s.wait_ge(h("swom"), t - 1)
                    s.activation(yf_sb[:, sl, :], ps_b[:], AF.Identity).then_inc(h("syte"), 1)

                def ln2_tail(t):
                    sl = t % 2
                    ti = t % GT
                    s.wait_ge(h("sbn3"), t + 1)
                    s.activation(sd_c[:, sl:sl + 1], mv_c[:, sl, 1:2], AF.Sqrt,
                                 bias=eps_col[:, 0:1]).then_inc(h("ssd3"), 1)
                    s.wait_ge(h("siv3"), t + 1)
                    if t >= 2:
                        s.wait_ge(h("sxn2t"), t - 1)
                    if t == 2:
                        s.wait_ge(h("se1"), 64)
                        s.wait_ge(h("se2"), 256)
                    s.activation(xn2_tm[:, sl, :], x1_g[:, ti, :], AF.Identity,
                                 bias=nmi_c[:, sl:sl + 1], scale=inv_c[:, sl:sl + 1]).then_inc(h("sxn2"), 1)

                sqrt2_o1(0)
                for g in range(NG):
                    for ti in range(GT):
                        t = g * GT + ti
                        if ti < GT - 1:
                            yf_ev(t)
                            sqrt2_o1(t + 1)
                            if t >= 1 and (t - 1) % GT != GT - 1:
                                xn2f_ev(t - 1)
                            ln2_tail(t)
                        else:
                            yf_ev(t)
                            xn2f_ev(t - 1)
                            ln2_tail(t)
                            xn2f_ev(t)
                            if t + 1 < NT:
                                sqrt2_o1(t + 1)
                    s.wait_ge(h("sd4"), g + 1)
                    if g >= 1:
                        s.wait_ge(h("sd4r"), 16 * g)
                    s.activation(d4p[:], ps_sm[0:4, 0:512], AF.Identity,
                                 scale=scol[0:4, 0:1]).then_inc(h("sd4e"), 1)
                    for fc in range(32):
                        s.wait_ge(h("sw1cc"), 32 * g + fc + 1)
                        s.activation(h_fm[:, fc, :], ps_b[:, (fc % 2) * 512:(fc % 2 + 1) * 512],
                                     AF.Gelu, bias=b1c_sb[:, fc:fc + 1]).then_inc(h("sgel"), 1)
                    s.wait_ge(h("shd"), g + 1)
                    if g >= 1:
                        s.wait_ge(h("sw2t"), 4 * g)
                    s.activation(hd_sb[:], ps_sm[0:16, 512:1024], AF.Identity,
                                 scale=scol[0:16, 0:1]).then_inc(h("shde"), 1)

            @blk2.tensor
            def _(t_):
                t_.wait_ge(h("ssbv"), 1)
                t_.wait_ge(h("wol"), 16)
                t_.wait_ge(h("brl"), 48)

                def o_mm(t, xnf=None):
                    sl = t % 2
                    t_.wait_ge(h("ql%d" % sl), 16 * (t // 2 + 1))
                    if t % GT == 1 and t > 1:
                        t_.wait_ge(h("sw2e"), t - 1)
                    if t >= 1:
                        t_.wait_ge(h("so1"), t)
                    if t >= 2:
                        t_.wait_ge(h("sxn2f"), t - 2 if xnf is None else xnf)
                    for hh in range(HN):
                        mm = t_.matmul(ps_a[:, hh * 64:(hh + 1) * 64],
                                       qpt[:, sl, hh * 128:(hh + 1) * 128],
                                       s_sb[:, hh * 64:(hh + 1) * 64],
                                       start=True, stop=True)
                    mm.then_inc(h("somm"), 1)

                o_mm(0)
                for g in range(NG):
                    for ti in range(GT):
                        t = g * GT + ti
                        sl = t % 2
                        if ti < GT - 1 and t + 1 < NT:
                            o_mm(t + 1)
                        if ti != 0 or g == 0:
                            t_.wait_ge(h("ssy"), t + 1)
                            if t >= 1:
                                t_.wait_ge(h("syte"), t)
                            for c in range(8):
                                mm = t_.matmul(ps_b[:, c * 128:(c + 1) * 128],
                                               y_sb[:, c * 128:(c + 1) * 128], ident[:],
                                               start=True, stop=True)
                            mm.then_inc(h("syt"), 1)
                        t_.wait_ge(h("syte"), t + 1)
                        if t >= 1:
                            t_.wait_ge(h("sx1e"), t)
                        if ti == 0 and g >= 1:
                            t_.wait_ge(h("sw2e"), 4 * g)
                        for seg in range(2):
                            for kc in range(8):
                                t_.matmul(ps_c[:, seg * 512:(seg + 1) * 512],
                                          yf_sb[:, sl, kc * 128:(kc + 1) * 128],
                                          wo_bf[:, kc * D + seg * 512:kc * D + (seg + 1) * 512],
                                          start=(kc == 0), stop=False)
                            mm = t_.matmul(ps_c[:, seg * 512:(seg + 1) * 512],
                                           ones_row[0:1, 0:128],
                                           bo_row[0:1, seg * 512:(seg + 1) * 512],
                                           start=False, stop=True)
                        mm.then_inc(h("swom"), 1)
                        if t >= 1 and (t - 1) % GT != GT - 1:
                            tm = t - 1
                            t_.wait_ge(h("sxn2"), tm + 1)
                            if ti < GT - 1:
                                t_.wait_ge(h("so1"), t + 2)
                            else:
                                t_.wait_ge(h("sxn2f"), tm)
                            for c in range(8):
                                mm = t_.matmul(ps_a[:, c * 128:(c + 1) * 128],
                                               xn2_tm[:, tm % 2, c * 128:(c + 1) * 128], ident[:],
                                               start=True, stop=True)
                            mm.then_inc(h("sxn2t"), 1)
                    tb = g * GT + GT - 1
                    t_.wait_ge(h("sxn2"), tb + 1)
                    t_.wait_ge(h("sxn2f"), tb)
                    for c in range(8):
                        mm = t_.matmul(ps_a[:, c * 128:(c + 1) * 128],
                                       xn2_tm[:, tb % 2, c * 128:(c + 1) * 128], ident[:],
                                       start=True, stop=True)
                    mm.then_inc(h("sxn2t"), 1)
                    if g + 1 < NG:
                        o_mm((g + 1) * GT, xnf=(g + 1) * GT)
                    t_.wait_ge(h("sxn2f"), 4 * (g + 1))
                    if g == 0:
                        t_.wait_ge(h("sodd"), 1)
                    if g >= 1:
                        t_.wait_ge(h("sd4e"), g)
                    for kc in range(8):
                        mm = t_.matmul(ps_sm[0:4, 0:512], odd1_sb[:, kc, :], xn2_fm[:, kc, :],
                                       start=(kc == 0), stop=(kc == 7))
                    mm.then_inc(h("sd4"), 1)
                    if g == 0:
                        t_.wait_ge(h("se1"), 64)
                    t_.wait_ge(h("syte"), 4 * (g + 1))
                    t_.wait_ge(h("sxn2f"), 4 * (g + 1))
                    for fc in range(32):
                        idx = 32 * g + fc
                        i_fc = fc // 8
                        t_.wait_ge(h("wl%d" % (idx % 3)), 16 * (idx // 3 + 1))
                        if idx >= 2:
                            t_.wait_ge(h("sgel"), idx - 1)
                        psl = ps_b[:, (fc % 2) * 512:(fc % 2 + 1) * 512]
                        for kc in range(8):
                            t_.matmul(psl, w1sl[:, idx % 3, kc * 128:(kc + 1) * 128],
                                      xn2_fm[:, kc, :], start=(kc == 0), stop=False)
                        if fc == 0:
                            t_.wait_ge(h("sd4r"), 16 * (g + 1))
                        mm = t_.matmul(psl, even1_sb[0:1, i_fc, (fc % 8) * 128:(fc % 8 + 1) * 128],
                                       d4row[0:1, i_fc * 512:(i_fc + 1) * 512],
                                       start=False, stop=True)
                        mm.then_inc(h("sw1cc"), 1)
                        if fc == 31:
                            t_.nop().then_inc(h("sw1mg"), 1)
                        if fc == 2 and g == 0:
                            t_.wait_ge(h("sodd"), 2)
                        if fc == 2 and g >= 1:
                            t_.wait_ge(h("shde"), g)
                        if fc >= 2:
                            kch = fc - 2
                            t_.wait_ge(h("sgel"), 32 * g + kch + 1)
                            t_.matmul(ps_sm[0:16, 512:1024], odd2_sb[:, kch, :], h_fm[:, kch, :],
                                      start=(kch == 0), stop=False)
                    if g == 0:
                        t_.wait_ge(h("w2l"), 32)
                        t_.wait_ge(h("se2"), 256)
                    for kch in range(30, 32):
                        t_.wait_ge(h("sgel"), 32 * g + kch + 1)
                        mm = t_.matmul(ps_sm[0:16, 512:1024], odd2_sb[:, kch, :], h_fm[:, kch, :],
                                       start=False, stop=(kch == 31))
                    mm.then_inc(h("shd"), 1)
                    if g + 1 < NG:
                        tn = (g + 1) * GT
                        t_.wait_ge(h("ssy"), tn + 1)
                        t_.wait_ge(h("syte"), tn)
                        t_.wait_ge(h("sgel"), 32 * (g + 1))
                        for c in range(8):
                            mm = t_.matmul(ps_b[:, c * 128:(c + 1) * 128],
                                           y_sb[:, c * 128:(c + 1) * 128], ident[:],
                                           start=True, stop=True)
                        mm.then_inc(h("syt"), 1)
                    t_.wait_ge(h("shde"), g + 1)
                    for ti in range(GT):
                        t = g * GT + ti
                        psw = ps_c if ti % 2 == 0 else ps_a
                        if ti == 0:
                            t_.wait_ge(h("sx1e"), 4 * (g + 1))
                        elif ti == 1:
                            t_.wait_ge(h("so1"), min(t + 4, NT))
                        else:
                            t_.wait_ge(h("sw2e"), t - 1)
                        for seg in range(2):
                            for kc in range(32):
                                t_.matmul(psw[:, seg * 512:(seg + 1) * 512],
                                          h_fm[:, kc, ti * 128:(ti + 1) * 128],
                                          w2_bf[:, kc * D + seg * 512:kc * D + (seg + 1) * 512],
                                          start=(kc == 0), stop=False)
                            t_.matmul(psw[:, seg * 512:(seg + 1) * 512],
                                      ones_row[0:1, 0:128],
                                      b2_row[0:1, seg * 512:(seg + 1) * 512],
                                      start=False, stop=False)
                            mm = t_.matmul(psw[:, seg * 512:(seg + 1) * 512],
                                           hd_sb[:, ti * 128:(ti + 1) * 128],
                                           even2_sb[:, seg * 512:(seg + 1) * 512],
                                           start=False, stop=True)
                        mm.then_inc(h("sw2t"), 1)

            blk2.__exit__(None, None, None)

    return nc


def _host_stage(inputs):
    x = np.ascontiguousarray(inputs["L0_attn_x"], dtype=np.float32)
    Wq = np.asarray(inputs["L0_attn_Wqkvu"], dtype=np.float32)
    Wo = np.asarray(inputs["L0_attn_Wout"], dtype=np.float32)
    W1 = np.asarray(inputs["L0_ffn_W1"], dtype=np.float32)
    W2 = np.asarray(inputs["L0_ffn_W2"], dtype=np.float32)

    def pmaj(wt):
        K, F = wt.shape
        nk = K // 128
        return np.ascontiguousarray(wt.reshape(nk, 128, F).transpose(1, 0, 2).reshape(128, nk * F))

    wq = pmaj(np.ascontiguousarray(Wq.T))
    wo = pmaj(np.ascontiguousarray(Wo.T))
    w2 = pmaj(np.ascontiguousarray(W2.T))
    w1 = np.ascontiguousarray(
        W1.reshape(32, 128, 8, 128).transpose(3, 0, 2, 1).reshape(128, 32 * D))
    rep = lambda a: np.ascontiguousarray(
        np.broadcast_to(np.asarray(a, dtype=np.float32).reshape(1, -1), (128, a.size)))
    tc = np.ascontiguousarray(
        np.arange(NT, dtype=np.float32)[None, :] * 128.0
        + np.arange(128, dtype=np.float32)[:, None])
    shared = {
        "wq": wq, "wo": wo, "w1": w1, "w2": w2,
        "bq": np.asarray(inputs["L0_attn_bqkvu"], dtype=np.float32).reshape(1, H4),
        "bo": np.asarray(inputs["L0_attn_bout"], dtype=np.float32).reshape(1, D),
        "b1": np.asarray(inputs["L0_ffn_b1"], dtype=np.float32).reshape(1, H4),
        "b1c": np.ascontiguousarray(
            np.asarray(inputs["L0_ffn_b1"], dtype=np.float32).reshape(32, 128).T),
        "b2": np.asarray(inputs["L0_ffn_b2"], dtype=np.float32).reshape(1, D),
        "lnw": rep(np.asarray(inputs["L0_attn_ln_w"])),
        "lnb": rep(np.asarray(inputs["L0_attn_ln_b"])),
        "th": rep(np.asarray(inputs["L0_attn_theta"]).reshape(-1)),
        "tc": tc,
        "sc": np.ascontiguousarray(np.broadcast_to(
            np.asarray(inputs["L0_ffn_scale"], dtype=np.float32).reshape(1, 1), (128, 1))),
        "idm": np.eye(128, dtype=np.float32),
    }
    shared = {k: np.ascontiguousarray(v, dtype=np.float32) for k, v in shared.items()}
    return [dict(shared, x=np.ascontiguousarray(x[b])) for b in range(8)]


def kernel(**inputs):
    in_maps = _host_stage(inputs)
    nc = build_nc()
    res = run_bass_kernel_spmd(nc, in_maps, core_ids=list(range(8)))
    out = np.stack([np.asarray(r["y"]) for r in res.results], axis=0)
    return out.astype(np.float32)
